# version 05
# baseline (speedup 1.0000x reference)
import numpy as np
from contextlib import ExitStack

import concourse.bass as bass
import concourse.mybir as mybir
from concourse.bass_utils import run_bass_kernel_spmd

F32 = mybir.dt.float32
F32R = mybir.dt.float32r
BF16 = mybir.dt.bfloat16
U32 = mybir.dt.uint32
AF = mybir.ActivationFunctionType
ALU = mybir.AluOpType
AXX = mybir.AxisListType.X

T = 4096
SA = 1024
NS = T // SA
SC = 512
NSC = T // SC
D, DI, DF = 1024, 2048, 4096
DC, IC, FC = 8, 16, 32
RCP1024 = float(np.float32(2.0 ** -10))
CBIG = float(np.float32(3 * 2 ** 22))
CLIP15 = 1.4999999
CCGROUPS = [[0, 1], [2, 3], [4, 5], [6, 7]]


class Prog:
    ENGINES = ("sync", "scalar", "vector", "gpsimd", "tensor")

    def __init__(self, nc, sems):
        self.nc = nc
        self.sems = sems
        self.ct = {k: 0 for k in sems}
        self.items = []

    def wait(self, eng, sem, val=None):
        if val is None:
            val = self.ct[sem]
        if val <= 0:
            return
        s = self.sems[sem]
        self.items.append((eng, lambda e, s=s, v=val: e.wait_ge(s, v)))

    def dma(self, eng, sem, out, in_):
        self.ct[sem] += 16
        s = self.sems[sem]
        self.items.append(
            (eng, lambda e, s=s, o=out, i=in_: e.dma_start(out=o, in_=i).then_inc(s, 16)))
        return self.ct[sem]

    SELF_SEM = {"vector": "vE", "scalar": "sE"}

    def run(self, eng, fn, sem=None):
        own = self.SELF_SEM.get(eng)
        if own is not None and self.ct[own] > 0:
            self.wait(eng, own, self.ct[own])
        if sem is None:
            self.items.append((eng, fn))
            return None
        self.ct[sem] += 1
        s = self.sems[sem]
        self.items.append((eng, lambda e, s=s, f=fn: f(e).then_inc(s, 1)))
        return self.ct[sem]

    def join(self, engines=None):
        snap = dict(self.ct)
        for eng in (engines or self.ENGINES):
            for sem, v in snap.items():
                if v > 0:
                    self.wait(eng, sem, v)

    def replay(self, block):
        per = {e: [fn for (ee, fn) in self.items if ee == e] for e in self.ENGINES}

        @block.sync
        def _(e):
            for fn in per["sync"]:
                fn(e)

        @block.scalar
        def _(e):
            for fn in per["scalar"]:
                fn(e)

        @block.vector
        def _(e):
            for fn in per["vector"]:
                fn(e)

        @block.gpsimd
        def _(e):
            for fn in per["gpsimd"]:
                fn(e)

        @block.tensor
        def _(e):
            for fn in per["tensor"]:
                fn(e)


def build_nc(tap=None):
    nc = bass.Bass("TRN2", target_bir_lowering=False, detect_race_conditions=False)

    def din(name, shape):
        return nc.dram_tensor(name, shape, F32, kind="ExternalInput").ap()

    x_h = din("x_fm", [D, T])
    w_in_h = din("w_in", [16, 128, 8 * 128])
    w_gate_h = din("w_gate", [16, 128, 8 * 128])
    w_b_h = din("w_b", [16, 128, 16 * 128])
    w_c_h = din("w_c", [16, 128, 16 * 128])
    w_d_h = din("w_d", [16, 128, 16 * 128])
    w_out_h = din("w_out", [8, 128, 16 * 128])
    w_fg_h = din("w_fg", [8, 128, 8 * 512])
    w_fu_h = din("w_fu", [8, 128, 8 * 512])
    w_fd_h = din("w_fd", [2, 128, 32 * 512])
    bias_h = din("biasp", [128, 112])
    mask_h = din("maskc", [128, 1])
    ident_h = din("ident_in", [128, 128])
    out_h = nc.dram_tensor("out_fm", [D, T], F32, kind="ExternalOutput").ap()

    def scr(name, shape, dt):
        return nc.dram_tensor(name, shape, dt, kind="Internal").ap()

    s_in = scr("s_in", [16, 128, 8 * 128], BF16)
    s_gate = scr("s_gate", [16, 128, 8 * 128], BF16)
    s_b = scr("s_b", [16, 128, 16 * 128], BF16)
    s_c = scr("s_c", [16, 128, 16 * 128], BF16)
    s_d = scr("s_d", [16, 128, 16 * 128], BF16)
    s_out = scr("s_out", [8, 128, 16 * 128], BF16)
    s_fg = scr("s_fg", [8, 128, 8 * 512], BF16)
    s_fu = scr("s_fu", [8, 128, 8 * 512], BF16)
    s_fd = scr("s_fd", [2, 128, 32 * 512], BF16)
    u_sp = scr("u_sp", [NS, 128, IC * SA], BF16)
    b_sp = scr("b_sp", [NS, IC, 128, SA], F32)
    x2_sp = scr("x2_sp", [NS, 128, DC * SA], F32)
    xq1_sp = scr("xq1_sp", [NS, 128, DC * SA], BF16)
    inv1_sp = scr("inv1_sp", [NS, SA], F32)
    cc_in = scr("cc_in", [128, 16], F32)
    cc_out = scr("cc_out", [256, 16], F32)

    AR_N = 44032
    arena = nc.alloc_sbuf_tensor("arena", [128, AR_N], F32).ap()

    def reg(off, n, dt=F32):
        assert off + n <= AR_N, (off, n)
        ap = arena[:, off:off + n]
        if dt == BF16:
            return ap.bitcast(BF16)
        return ap

    X_T = reg(0, 8192)
    XQ1_T = reg(0, 4096, BF16)
    U_T = reg(8192, 8192, BF16)
    XB_T = reg(8192, 4096, BF16)
    XSQ_T = reg(12288, 4096, BF16)
    XN_T = reg(16384, 4096, BF16)
    Y_T = reg(20480, 8192, BF16)
    YQF_T = reg(20480, 8192)
    X2_T = reg(28672, 8192)
    WS = [reg(36864 + i * 1024, 1024, BF16) for i in range(4)]
    BW = [reg(40960 + i * 1024, 1024) for i in range(2)]
    SCO_A = reg(20480, 1024)
    TMPA_A = [reg(21504, 1024), reg(22528, 1024)]
    SCO_B = reg(16384, 1024)
    TMPA_B = [reg(17408, 1024), reg(18432, 1024)]
    TMP3_B = reg(19456, 1024)
    MAXC = reg(36864, 64)
    PT_F = [reg(0, 4096), reg(4096, 4096)]
    PT_B = [reg(8192, 2048, BF16), reg(10240, 2048, BF16)]
    C_WF = [reg(i * 2048, 2048, BF16) for i in range(4)]
    C_XQ1 = reg(8192, 2048, BF16)
    C_GU = reg(10240, 16384)
    C_WD = [reg(10240, 4096, BF16),
            reg(14336, 4096, BF16)]
    C_XQ2 = reg(26624, 8192, BF16)
    C_X2 = reg(34816, 4096)
    C_BC = reg(38912, 1536)
    C_TMP = reg(40448, 512)
    C_GS = reg(40960, 512)
    C_UP = reg(41472, 512)
    MAXC_C = reg(41984, 64)

    rows = nc.alloc_sbuf_tensor("rows", [1, 4 * 1024], F32).ap()
    R = [rows[0:1, i * 1024:(i + 1) * 1024] for i in range(4)]
    rows_r = nc.alloc_sbuf_tensor("rows_r", [1, 3 * 1024], F32R).ap()
    RR = [rows_r[0:1, i * 1024:(i + 1) * 1024] for i in range(3)]

    bias_t = nc.alloc_sbuf_tensor("bias_t", [128, 112], F32).ap()
    IN_B, GATE_B = bias_t[:, 0:16], bias_t[:, 16:32]
    B_B, C_B, D_B = bias_t[:, 32:48], bias_t[:, 48:64], bias_t[:, 64:80]
    OUT_B, NW, ALOG = bias_t[:, 80:88], bias_t[:, 88:96], bias_t[:, 96:112]
    nnw_t = nc.alloc_sbuf_tensor("nnw", [128, 8], F32).ap()
    a_t = nc.alloc_sbuf_tensor("a_t", [128, 16], F32).ap()
    a_tmp = nc.alloc_sbuf_tensor("a_tmp", [128, 64], F32).ap()
    a_msk = nc.alloc_sbuf_tensor("a_msk", [128, 16], U32).ap()
    state_t = nc.alloc_sbuf_tensor("state", [128, 16], F32).ap()
    hprev_t = nc.alloc_sbuf_tensor("hprev", [128, 16], F32).ap()
    mask_t = nc.alloc_sbuf_tensor("maskt", [128, 1], F32).ap()
    ones_bf = nc.alloc_sbuf_tensor("ones_bf", [128, 2], BF16).ap()
    ones_r = nc.alloc_sbuf_tensor("ones_r", [1, 128], F32R).ap()
    ones_f = nc.alloc_sbuf_tensor("ones_f", [1, 128], F32).ap()
    onec_f = nc.alloc_sbuf_tensor("onec_f", [128, 2], F32).ap()
    ident = nc.alloc_sbuf_tensor("ident", [128, 128], F32).ap()
    mw_t = nc.alloc_sbuf_tensor("mw_t", [128, 8], F32).ap()
    wsc_t = nc.alloc_sbuf_tensor("wsc_t", [1, 8], F32).ap()
    acc_t = nc.alloc_sbuf_tensor("acc_t", [128, 8], F32).ap()
    inv1_row = nc.alloc_sbuf_tensor("inv1_row", [1, 512], F32).ap()

    psA = nc.alloc_psum_tensor("psA", [128, 2048], F32).ap()
    psB = nc.alloc_psum_tensor("psB", [128, 2048], F32).ap()

    es = ExitStack()
    semnames = ["sq", "aq", "gq", "tE", "vE", "sE", "cc"]
    sems = {k: es.enter_context(nc.semaphore(k)) for k in semnames}
    p = Prog(nc, sems)
    tap_state = {"done": False}

    def tap_out(name, ap, shape, dt=F32):
        if tap != name or tap_state["done"]:
            return
        tap_state["done"] = True
        th = nc.dram_tensor("dbg", list(shape), dt, kind="ExternalOutput").ap()
        p.join()
        p.dma("sync", "sq", th[:], ap)
        p.join()

    def emit_rsqrt(src, dst, scratch):
        p.run("scalar", lambda e: e.activation(scratch, src, AF.Ln), "sE")
        p.run("scalar", lambda e: e.activation(dst, scratch, AF.Exp, scale=-0.5), "sE")
        p.join(["vector"])
        for _ in range(2):
            p.run("vector", lambda e: e.tensor_tensor(scratch, dst, dst, ALU.mult), "vE")
            p.run("vector", lambda e: e.tensor_tensor(scratch, scratch, src, ALU.mult), "vE")
            p.run("vector", lambda e: e.tensor_scalar(
                scratch, scratch, -0.5, 1.5, ALU.mult, ALU.add), "vE")
            p.run("vector", lambda e: e.tensor_tensor(dst, dst, scratch, ALU.mult), "vE")

    def emit_stats(xb, xsq, var_mult, var_eps, n):
        for src, base in ((xb, 0), (xsq, 1024)):
            for k in range(DC):
                for h in range(n // 512):
                    p.run("tensor", lambda e, s=src, k=k, h=h, b=base: e.matmul(
                        psA[0:1, b + h * 512:b + (h + 1) * 512],
                        ones_bf[:, 0:1],
                        s[:, k * n + h * 512:k * n + (h + 1) * 512],
                        start=(k == 0), stop=(k == DC - 1)), "tE")
        p.wait("vector", "tE")
        p.run("vector", lambda e: e.tensor_scalar(
            R[0][0:1, 0:n], psA[0:1, 0:n], RCP1024, None, ALU.mult), "vE")
        p.run("vector", lambda e: e.tensor_scalar(
            R[1][0:1, 0:n], psA[0:1, 1024:1024 + n], RCP1024, None, ALU.mult), "vE")
        p.run("vector", lambda e: e.tensor_tensor(
            R[3][0:1, 0:n], R[0][0:1, 0:n], R[0][0:1, 0:n], ALU.mult), "vE")
        p.run("vector", lambda e: e.tensor_tensor(
            R[1][0:1, 0:n], R[1][0:1, 0:n], R[3][0:1, 0:n], ALU.subtract), "vE")
        p.run("vector", lambda e: e.tensor_scalar(
            R[1][0:1, 0:n], R[1][0:1, 0:n], var_mult, var_eps, ALU.mult, ALU.add), "vE")
        p.join(["scalar"])
        emit_rsqrt(R[1][0:1, 0:n], R[2][0:1, 0:n], R[3][0:1, 0:n])

    def emit_bcast(ps, pb_off, row_r, n):
        for h in range(n // 512):
            p.run("tensor", lambda e, h=h: e.matmul(
                ps[:, pb_off + h * 512:pb_off + (h + 1) * 512],
                ones_r[0:1, :], row_r[0:1, h * 512:(h + 1) * 512],
                start=True, stop=True), "tE")

    p.dma("sync", "sq", bias_t[:], bias_h[:])
    p.dma("sync", "sq", mask_t[:], mask_h[:])
    p.dma("sync", "sq", ident[:], ident_h[:])
    for (dst, src) in [(s_in, w_in_h), (s_gate, w_gate_h), (s_b, w_b_h),
                       (s_c, w_c_h), (s_d, w_d_h), (s_out, w_out_h)]:
        p.dma("gpsimd", "gq", dst[:], src[:])
    for mset in [ones_bf, ones_f, onec_f]:
        p.run("vector", lambda e, m=mset: e.memset(m[:], 1.0), "vE")
    p.run("vector", lambda e: e.tensor_copy(ones_r[:], ones_f[:]), "vE")
    p.run("vector", lambda e: e.memset(state_t[:], 0.0), "vE")
    p.run("vector", lambda e: e.memset(acc_t[:], 0.0), "vE")
    p.wait("vector", "sq", 16)
    p.run("vector", lambda e: e.tensor_scalar(nnw_t[:], NW, -1.0, None, ALU.mult), "vE")

    AT = [a_tmp[:, i * 16:(i + 1) * 16] for i in range(4)]
    p.join(["scalar"])
    p.run("scalar", lambda e: e.activation(AT[0], ALOG, AF.Abs), "sE")
    p.run("scalar", lambda e: e.activation(AT[0], AT[0], AF.Exp, scale=-1.0), "sE")
    p.run("scalar", lambda e: e.activation(AT[1], AT[0], AF.Ln, bias=1.0), "sE")
    p.join(["vector"])
    p.run("vector", lambda e: e.tensor_scalar(AT[2], AT[0], -0.25, 1.0 / 3.0, ALU.mult, ALU.add), "vE")
    p.run("vector", lambda e: e.tensor_tensor(AT[2], AT[2], AT[0], ALU.mult), "vE")
    p.run("vector", lambda e: e.tensor_scalar(AT[2], AT[2], -0.5, None, ALU.add), "vE")
    p.run("vector", lambda e: e.tensor_tensor(AT[2], AT[2], AT[0], ALU.mult), "vE")
    p.run("vector", lambda e: e.tensor_scalar(AT[2], AT[2], 1.0, None, ALU.add), "vE")
    p.run("vector", lambda e: e.tensor_tensor(AT[2], AT[2], AT[0], ALU.mult), "vE")
    p.run("vector", lambda e: e.tensor_scalar(a_msk[:], AT[0], 0.01, None, ALU.is_lt), "vE")
    p.run("vector", lambda e: e.select(AT[3], a_msk[:], AT[2], AT[1]), "vE")
    p.run("vector", lambda e: e.tensor_scalar(AT[0], ALOG, 0.0, None, ALU.max), "vE")
    p.run("vector", lambda e: e.tensor_tensor(AT[3], AT[3], AT[0], ALU.add), "vE")
    p.join(["scalar"])
    p.run("scalar", lambda e: e.activation(a_t[:], AT[3], AF.Exp, scale=-1.0), "sE")
    tap_out("a", a_t[:], [128, 16])

    ffn_tiles = []
    for (src, dst, mi) in [(w_fg_h, s_fg, 0), (w_fu_h, s_fu, 1), (w_fd_h, s_fd, 2)]:
        if mi < 2:
            for gi in range(8):
                ffn_tiles.append((src[gi], dst[gi], mi))
        else:
            for gi in range(2):
                for q in range(4):
                    ffn_tiles.append((src[gi][:, q * 4096:(q + 1) * 4096],
                                      dst[gi][:, q * 4096:(q + 1) * 4096], mi))
    prev_v = 0
    for i, (sap, dap, mi) in enumerate(ffn_tiles):
        par = i % 2
        if i >= 2:
            p.wait("sync", "vE", prev_v)
        lt = p.dma("sync", "sq", PT_F[par][:], sap)
        p.wait("vector", "sq", lt)
        p.run("vector", lambda e, t=PT_F[par]: e.reduce_sum(
            acc_t[:, 6:7], t[:], axis=AXX, apply_absolute_value=True), "vE")
        prev_v = p.run("vector", lambda e, m=mi: e.tensor_tensor(
            acc_t[:, m:m + 1], acc_t[:, m:m + 1], acc_t[:, 6:7], ALU.add), "vE")
    p.join(["tensor"])
    for mi in range(3):
        p.run("tensor", lambda e, m=mi: e.matmul(
            psA[0:1, m * 512:m * 512 + 1], onec_f[:, 0:1], acc_t[:, m:m + 1],
            start=True, stop=True), "tE")
    p.wait("vector", "tE")
    for mi in range(3):
        p.run("vector", lambda e, m=mi: e.tensor_scalar(
            wsc_t[0:1, m:m + 1], psA[0:1, m * 512:m * 512 + 1],
            float(np.float32(2.0 ** -22)), 1e-5, ALU.mult, ALU.max), "vE")
    p.join(["tensor"])
    for mi in range(3):
        p.run("tensor", lambda e, m=mi: e.matmul(
            psB[:, m * 512:m * 512 + 1], ones_f[0:1, :], wsc_t[0:1, m:m + 1],
            start=True, stop=True), "tE")
    p.wait("vector", "tE")
    for mi in range(3):
        p.run("vector", lambda e, m=mi: e.tensor_copy(
            mw_t[:, m:m + 1], psB[:, m * 512:m * 512 + 1]), "vE")
    for mi in range(3):
        p.run("vector", lambda e, m=mi: e.reciprocal(
            mw_t[:, 3 + m:4 + m], mw_t[:, m:m + 1]), "vE")
    p.join()

    prev_v = 0
    prev_g = 0
    for i, (sap, dap, mi) in enumerate(ffn_tiles):
        par = i % 2
        if i >= 2:
            p.wait("sync", "vE", prev_v)
        lt = p.dma("sync", "sq", PT_F[par][:], sap)
        p.wait("vector", "sq", lt)
        if i >= 2:
            p.wait("vector", "gq", prev_g)
        ws = mw_t[:, 3 + mi:4 + mi]
        p.run("vector", lambda e, t=PT_F[par], w=ws: e.tensor_scalar(
            t[:], t[:], w, CLIP15, ALU.mult, ALU.min), "vE")
        p.run("vector", lambda e, t=PT_F[par]: e.tensor_scalar(
            t[:], t[:], -CLIP15, CBIG, ALU.max, ALU.add), "vE")
        prev_v = p.run("vector", lambda e, t=PT_F[par], o=PT_B[par]: e.tensor_scalar(
            o[:], t[:], CBIG, None, ALU.subtract), "vE")
        p.wait("gpsimd", "vE", prev_v)
        prev_g = p.dma("gpsimd", "gq", dap, PT_B[par][:])
    p.join()

    x_r = x_h.rearrange("(c p) t -> p c t", p=128)
    X3 = X_T.rearrange("p (c t) -> p c t", c=DC)
    XB3 = XB_T.rearrange("p (c t) -> p c t", c=DC)

    def emit_ln1_head(s, need_xn, TMPA):
        xsl = x_r[:, :, s * SA:(s + 1) * SA]
        p.dma("sync", "sq", X3[:, :, :], xsl)
        p.dma("gpsimd", "gq", XB3[:, :, :], xsl)
        p.wait("scalar", "sq")
        p.wait("scalar", "gq")
        p.run("scalar", lambda e: e.activation(XSQ_T[:], X_T[:], AF.Square), "sE")
        p.wait("tensor", "gq")
        p.wait("tensor", "sE")
        if need_xn:
            emit_stats(XB_T, XSQ_T, 1.000001, 1e-11, SA)
            p.run("vector", lambda e: e.tensor_copy(RR[0][0:1, 0:SA], R[2][0:1, 0:SA]), "vE")
            p.run("vector", lambda e: e.tensor_tensor(
                R[3][0:1, 0:SA], R[0][0:1, 0:SA], R[2][0:1, 0:SA], ALU.mult), "vE")
            p.run("vector", lambda e: e.tensor_copy(RR[1][0:1, 0:SA], R[3][0:1, 0:SA]), "vE")
            p.join(["tensor"])
            emit_bcast(psB, 0, RR[0], SA)
            emit_bcast(psB, 1024, RR[1], SA)
            p.wait("vector", "tE")
            for c in range(DC):
                p.run("vector", lambda e, c=c: e.scalar_tensor_tensor(
                    TMPA[0][:], X_T[:, c * SA:(c + 1) * SA], NW[:, c:c + 1],
                    psB[:, 0:SA], ALU.mult, ALU.mult), "vE")
                p.run("vector", lambda e, c=c: e.scalar_tensor_tensor(
                    XN_T[:, c * SA:(c + 1) * SA], psB[:, 1024:1024 + SA],
                    nnw_t[:, c:c + 1], TMPA[0][:], ALU.mult, ALU.add), "vE")
        else:
            emit_stats(XB_T, XSQ_T, 1.0, 1e-5, SA)
            p.run("vector", lambda e: e.tensor_tensor(
                R[3][0:1, 0:SA], R[0][0:1, 0:SA], R[2][0:1, 0:SA], ALU.mult), "vE")
            p.run("vector", lambda e: e.tensor_copy(RR[1][0:1, 0:SA], R[3][0:1, 0:SA]), "vE")
            p.run("vector", lambda e: e.tensor_scalar(
                R[3][0:1, 0:SA], R[2][0:1, 0:SA], 1.0, None, ALU.add), "vE")
            p.run("vector", lambda e: e.tensor_copy(RR[2][0:1, 0:SA], R[3][0:1, 0:SA]), "vE")

    def emit_mm_loop(nm, kc, wsrc_fn, rhs, n, evac_fn):
        dmatok = [0] * nm
        mmtok = [0] * nm
        etok = [None] * nm
        for m in range(min(2, nm)):
            dmatok[m] = p.dma("sync", "sq", WS[m % 2][:, 0:kc * 128], wsrc_fn(m))
        for m in range(nm):
            par = m % 2
            ps = psA if par == 0 else psB
            p.wait("tensor", "sq", dmatok[m])
            if m >= 2 and etok[m - 2] is not None:
                p.wait("tensor", etok[m - 2][0], etok[m - 2][1])
            tok = None
            for k in range(kc):
                for h in range(n // 512):
                    tok = p.run("tensor", lambda e, ps=ps, k=k, h=h, par=par: e.matmul(
                        ps[:, h * 512:(h + 1) * 512],
                        WS[par][:, k * 128:(k + 1) * 128],
                        rhs[:, k * n + h * 512:k * n + (h + 1) * 512],
                        start=(k == 0), stop=(k == kc - 1)), "tE")
            mmtok[m] = tok
            if m + 2 < nm:
                p.wait("sync", "tE", mmtok[m])
                dmatok[m + 2] = p.dma("sync", "sq", WS[par][:, 0:kc * 128], wsrc_fn(m + 2))
            etok[m] = evac_fn(m, ps)
        return mmtok

    def emit_mm_loop2(nm, kc, wsrc1, wsrc2, rhs, n, evac_fn, pre_fn=None):
        dmatok = [[0] * nm, [0] * nm]
        mmtok = [0] * nm
        etok = [None] * nm
        for m in range(min(2, nm)):
            dmatok[0][m] = p.dma("sync", "sq", WS[m % 2][:, 0:kc * 128], wsrc1(m))
            dmatok[1][m] = p.dma("sync", "sq", WS[2 + m % 2][:, 0:kc * 128], wsrc2(m))
        for m in range(nm):
            par = m % 2
            ps = psA if par == 0 else psB
            if pre_fn is not None:
                pre_fn(m)
            p.wait("tensor", "sq", max(dmatok[0][m], dmatok[1][m]))
            if m >= 2 and etok[m - 2] is not None:
                p.wait("tensor", etok[m - 2][0], etok[m - 2][1])
            tok = None
            for wi, wb in ((0, WS[par]), (1, WS[2 + par])):
                for k in range(kc):
                    for h in range(n // 512):
                        tok = p.run("tensor", lambda e, ps=ps, k=k, h=h, wb=wb, wi=wi: e.matmul(
                            ps[:, wi * 1024 + h * 512:wi * 1024 + (h + 1) * 512],
                            wb[:, k * 128:(k + 1) * 128],
                            rhs[:, k * n + h * 512:k * n + (h + 1) * 512],
                            start=(k == 0), stop=(k == kc - 1)), "tE")
            mmtok[m] = tok
            if m + 2 < nm:
                p.wait("sync", "tE", mmtok[m])
                dmatok[0][m + 2] = p.dma("sync", "sq", WS[par][:, 0:kc * 128], wsrc1(m + 2))
                dmatok[1][m + 2] = p.dma("sync", "sq", WS[2 + par][:, 0:kc * 128], wsrc2(m + 2))
            etok[m] = evac_fn(m, ps)
        return mmtok

    for s in range(NS):
        p.join()
        emit_ln1_head(s, True, TMPA_A)
        tap_out("xn" if s == 0 else "_xn%d" % s, XN_T[:], [128, 8192], BF16)
        p.join()

        def evac_ug(m, ps):
            par = m % 2
            p.wait("scalar", "tE")
            p.run("scalar", lambda e, ps=ps, m=m, par=par: e.activation(
                TMPA_A[par][:], ps[:, 0:SA], AF.Identity, bias=IN_B[:, m:m + 1]), "sE")
            tok = p.run("scalar", lambda e, ps=ps, m=m, par=par: e.activation(
                BW[par][:], ps[:, 1024:1024 + SA], AF.Sigmoid, bias=GATE_B[:, m:m + 1]), "sE")
            p.wait("vector", "sE")
            p.run("vector", lambda e, m=m, par=par: e.tensor_tensor(
                U_T[:, m * SA:(m + 1) * SA], TMPA_A[par][:], BW[par][:], ALU.mult), "vE")
            return ("sE", tok)

        emit_mm_loop2(IC, DC, lambda m: s_in[m], lambda m: s_gate[m],
                      XN_T, SA, evac_ug)
        p.join()
        tap_out("u" if s == 0 else "_u%d" % s, U_T[:], [128, 16384], BF16)
        p.dma("gpsimd", "gq", u_sp[s], U_T[:])

        def evac_b(m, ps, s=s):
            par = m % 2
            if m >= 2:
                p.wait("scalar", "vE")
                p.wait("scalar", "gq")
            p.wait("scalar", "tE")
            tok = p.run("scalar", lambda e, ps=ps, m=m, par=par: e.activation(
                BW[par][:], ps[:, 0:SA], AF.Identity, bias=B_B[:, m:m + 1]), "sE")
            p.wait("gpsimd", "sE")
            p.dma("gpsimd", "gq", b_sp[s, m], BW[par][:])
            p.wait("vector", "sE")
            p.run("vector", lambda e, m=m, par=par: e.tensor_tensor_scan(
                SCO_A[:], a_t[:, m:m + 1].broadcast_to((128, SA)), BW[par][:],
                state_t[:, m:m + 1], ALU.mult, ALU.add), "vE")
            p.run("vector", lambda e, m=m: e.tensor_copy(
                state_t[:, m:m + 1], SCO_A[:, SA - 1:SA]), "vE")
            return ("sE", tok)

        emit_mm_loop(IC, IC, lambda m: s_b[m], U_T, SA, evac_b)
        p.join()
    tap_out("hend", state_t[:], [128, 16])

    p.join()
    p.dma("gpsimd", "gq", cc_in[:], state_t[:])
    p.wait("gpsimd", "gq")
    p.run("gpsimd", lambda e: e.collective_compute(
        "AllGather", ALU.bypass,
        replica_groups=CCGROUPS,
        ins=[cc_in[:]], outs=[cc_out[:]]), "cc")
    p.wait("gpsimd", "cc")
    p.dma("gpsimd", "gq", hprev_t[:], cc_out[0:128, :])
    p.wait("vector", "gq")
    p.run("vector", lambda e: e.tensor_scalar(
        state_t[:], hprev_t[:], mask_t[:, 0:1], None, ALU.mult), "vE")
    p.join()
    tap_out("hseed", state_t[:], [128, 16])

    for s in range(NS):
        p.join()
        emit_ln1_head(s, False, TMPA_B)
        p.join()
        p.dma("sync", "sq", U_T[:], u_sp[s])

        btok = {}

        def pre_cd(m, s=s, btok=btok):
            if m == 0:
                btok[0] = p.dma("sync", "aq", BW[0][:], b_sp[s, 0])
            if m + 1 < IC:
                p.wait("sync", "vE")
                btok[m + 1] = p.dma("sync", "aq", BW[(m + 1) % 2][:], b_sp[s, m + 1])
            p.wait("vector", "aq", btok[m])
            p.run("vector", lambda e, m=m: e.tensor_tensor_scan(
                SCO_B[:], a_t[:, m:m + 1].broadcast_to((128, SA)), BW[m % 2][:],
                state_t[:, m:m + 1], ALU.mult, ALU.add), "vE")
            p.run("vector", lambda e, m=m: e.tensor_copy(
                state_t[:, m:m + 1], SCO_B[:, SA - 1:SA]), "vE")

        def evac_cd(m, ps):
            par = m % 2
            p.wait("scalar", "tE")
            p.run("scalar", lambda e, ps=ps, m=m, par=par: e.activation(
                TMPA_B[par][:], ps[:, 0:SA], AF.Identity, bias=C_B[:, m:m + 1]), "sE")
            tok = p.run("scalar", lambda e, ps=ps, m=m: e.activation(
                TMP3_B[:], ps[:, 1024:1024 + SA], AF.Identity, bias=D_B[:, m:m + 1]), "sE")
            p.wait("vector", "sE")
            p.run("vector", lambda e, par=par: e.tensor_tensor(
                TMPA_B[par][:], TMPA_B[par][:], SCO_B[:], ALU.mult), "vE")
            p.run("vector", lambda e, m=m, par=par: e.tensor_tensor(
                Y_T[:, m * SA:(m + 1) * SA], TMPA_B[par][:], TMP3_B[:], ALU.add), "vE")
            return ("sE", tok)

        p.wait("tensor", "sq")
        emit_mm_loop2(IC, IC, lambda m: s_c[m], lambda m: s_d[m],
                      U_T, SA, evac_cd, pre_fn=pre_cd)
        p.join()
        tap_out("y" if s == 0 else "_y%d" % s, Y_T[:], [128, 16384], BF16)

        def evac_out(m, ps):
            p.wait("scalar", "tE")
            tok = p.run("scalar", lambda e, ps=ps, m=m: e.activation(
                X2_T[:, m * SA:(m + 1) * SA], ps[:, 0:SA], AF.Identity,
                bias=OUT_B[:, m:m + 1]), "sE")
            return ("sE", tok)

        emit_mm_loop(DC, IC, lambda m: s_out[m], Y_T, SA, evac_out)
        p.join()

        emit_bcast(psB, 0, RR[2], SA)
        emit_bcast(psB, 1024, RR[1], SA)
        p.wait("vector", "tE")
        for c in range(DC):
            p.run("vector", lambda e, c=c: e.tensor_tensor(
                TMPA_B[0][:], X_T[:, c * SA:(c + 1) * SA], psB[:, 0:SA], ALU.mult), "vE")
            p.run("vector", lambda e, c=c: e.scalar_tensor_tensor(
                TMPA_B[0][:], psB[:, 1024:1024 + SA], -1.0, TMPA_B[0][:],
                ALU.mult, ALU.add), "vE")
            p.run("vector", lambda e, c=c: e.tensor_tensor(
                X2_T[:, c * SA:(c + 1) * SA], X2_T[:, c * SA:(c + 1) * SA],
                TMPA_B[0][:], ALU.add), "vE")
        p.join()
        tap_out("x2" if s == 0 else "_x2%d" % s, X2_T[:], [128, 8192])
        p.dma("gpsimd", "gq", x2_sp[s], X2_T[:])

        p.run("vector", lambda e: e.tensor_copy(XB_T[:], X2_T[:]), "vE")
        p.run("scalar", lambda e: e.activation(XSQ_T[:], X2_T[:], AF.Square), "sE")
        p.join(["tensor"])
        emit_stats(XB_T, XSQ_T, 1.0, 1e-5, SA)
        p.run("vector", lambda e: e.tensor_copy(RR[0][0:1, 0:SA], R[0][0:1, 0:SA]), "vE")
        p.join(["tensor"])
        emit_bcast(psB, 0, RR[0], SA)
        p.wait("vector", "tE")
        for c in range(DC):
            p.run("vector", lambda e, c=c: e.scalar_tensor_tensor(
                X_T[:, c * SA:(c + 1) * SA], psB[:, 0:SA], -1.0,
                X2_T[:, c * SA:(c + 1) * SA], ALU.mult, ALU.add), "vE")
        p.join(["tensor"])
        slots = [1024, 1536]
        vtok = 0
        for blk in range(SA // 128):
            for c in range(DC):
                i = blk * DC + c
                sl = slots[i % 2]
                if i >= 2:
                    p.wait("tensor", "vE", vtok)
                p.run("tensor", lambda e, sl=sl, c=c, blk=blk: e.transpose(
                    psB[:, sl:sl + 128],
                    X_T[:, c * SA + blk * 128:c * SA + (blk + 1) * 128],
                    ident[:]), "tE")
                p.wait("vector", "tE")
                vtok = p.run("vector", lambda e, sl=sl, c=c: e.reduce_max(
                    MAXC[:, 8 + c:9 + c], psB[:, sl:sl + 128], axis=AXX,
                    apply_absolute_value=True), "vE")
            p.run("vector", lambda e, blk=blk: e.reduce_max(
                MAXC[:, blk:blk + 1], MAXC[:, 8:16], axis=AXX), "vE")
        p.join(["tensor"])
        for blk in range(SA // 128):
            sl = slots[blk % 2]
            if blk >= 2:
                p.wait("tensor", "vE")
            p.run("tensor", lambda e, sl=sl, blk=blk: e.transpose(
                psB[0:1, sl:sl + 128], MAXC[:, blk:blk + 1], ident[:]), "tE")
            p.wait("vector", "tE")
            p.run("vector", lambda e, sl=sl, blk=blk: e.tensor_copy(
                R[3][0:1, blk * 128:(blk + 1) * 128], psB[0:1, sl:sl + 128]), "vE")
        p.run("vector", lambda e: e.tensor_tensor(
            R[3][0:1, 0:SA], R[3][0:1, 0:SA], R[2][0:1, 0:SA], ALU.mult), "vE")
        p.run("vector", lambda e: e.tensor_scalar(
            R[3][0:1, 0:SA], R[3][0:1, 0:SA], 1e-5, None, ALU.max), "vE")
        p.run("vector", lambda e: e.tensor_scalar(
            R[1][0:1, 0:SA], R[3][0:1, 0:SA], float(np.float32(1.0 / 127.0)),
            None, ALU.mult), "vE")
        p.wait("gpsimd", "vE")
        sp_tok = p.dma("gpsimd", "gq", inv1_sp[s:s + 1, :], R[1][0:1, 0:SA])
        p.wait("vector", "gq", sp_tok)
        p.run("vector", lambda e: e.reciprocal(R[1][0:1, 0:SA], R[3][0:1, 0:SA]), "vE")
        p.run("vector", lambda e: e.tensor_scalar(
            R[1][0:1, 0:SA], R[1][0:1, 0:SA], 127.0, None, ALU.mult), "vE")
        p.run("vector", lambda e: e.tensor_tensor(
            R[1][0:1, 0:SA], R[1][0:1, 0:SA], R[2][0:1, 0:SA], ALU.mult), "vE")
        p.run("vector", lambda e: e.tensor_copy(RR[1][0:1, 0:SA], R[1][0:1, 0:SA]), "vE")
        p.join(["tensor"])
        emit_bcast(psB, 0, RR[1], SA)
        p.wait("vector", "tE")
        for c in range(DC):
            p.run("vector", lambda e, c=c: e.tensor_tensor(
                YQF_T[:, c * SA:(c + 1) * SA], X_T[:, c * SA:(c + 1) * SA],
                psB[:, 0:SA], ALU.mult), "vE")
            p.run("vector", lambda e, c=c: e.tensor_scalar(
                XQ1_T[:, c * SA:(c + 1) * SA], YQF_T[:, c * SA:(c + 1) * SA],
                CBIG, CBIG, ALU.add, ALU.subtract), "vE")
        p.join(["gpsimd"])
        p.dma("gpsimd", "gq", xq1_sp[s], XQ1_T[:])
        tap_out("xq1" if s == 0 else "_xq1%d" % s, XQ1_T[:], [128, 8192], BF16)

    for sc in range(NSC):
        s, t0 = sc // 2, (sc % 2) * SC
        p.join()
        xq1s = xq1_sp[s].rearrange("p (c t) -> p c t", c=DC)[:, :, t0:t0 + SC]
        x2s = x2_sp[s].rearrange("p (c t) -> p c t", c=DC)[:, :, t0:t0 + SC]
        p.dma("sync", "sq", C_XQ1.rearrange("p (c t) -> p c t", c=DC), xq1s)
        p.dma("sync", "sq", C_X2.rearrange("p (c t) -> p c t", c=DC), x2s)
        p.dma("sync", "aq", inv1_row[0:1, 0:SC], inv1_sp[s:s + 1, t0:t0 + SC])
        p.wait("vector", "aq")
        p.run("vector", lambda e: e.tensor_copy(RR[0][0:1, 0:SC], inv1_row[0:1, 0:SC]), "vE")
        p.join(["tensor"])
        emit_bcast(psA, 0, RR[0], SC)
        p.wait("vector", "tE")
        p.run("vector", lambda e: e.tensor_copy(C_BC[:, 0:SC], psA[:, 0:SC]), "vE")
        p.join()

        gdma = [[0] * 8, [0] * 8]
        for gi in range(2):
            gdma[0][gi] = p.dma("sync", "sq", C_WF[gi % 2][:], s_fg[gi])
            gdma[1][gi] = p.dma("sync", "aq", C_WF[2 + gi % 2][:], s_fu[gi])
        evtok = 0
        for gi in range(8):
            par = gi % 2
            if gi >= 1:
                p.wait("tensor", "vE", evtok)
            p.wait("tensor", "sq", gdma[0][gi])
            p.wait("tensor", "aq", gdma[1][gi])
            tok = None
            for wb, ps in ((C_WF[par], psA), (C_WF[2 + par], psB)):
                for k in range(DC):
                    for ms in range(4):
                        tok = p.run("tensor", lambda e, ps=ps, k=k, ms=ms, wb=wb: e.matmul(
                            ps[:, ms * 512:(ms + 1) * 512],
                            wb[:, k * 512 + ms * 128:k * 512 + (ms + 1) * 128],
                            C_XQ1[:, k * SC:(k + 1) * SC],
                            start=(k == 0), stop=(k == DC - 1)), "tE")
            if gi + 2 < 8:
                p.wait("sync", "tE", tok)
                p.wait("sync", "vE", evtok)
                gdma[0][gi + 2] = p.dma("sync", "sq", C_WF[par][:], s_fg[gi + 2])
                gdma[1][gi + 2] = p.dma("sync", "aq", C_WF[2 + par][:], s_fu[gi + 2])
            p.wait("vector", "tE", tok)
            for ms in range(4):
                p.run("vector", lambda e, ms=ms: e.scalar_tensor_tensor(
                    C_UP[:], psA[:, ms * 512:(ms + 1) * 512],
                    mw_t[:, 0:1], C_BC[:, 0:SC], ALU.mult, ALU.mult), "vE")
                p.wait("scalar", "vE")
                p.run("scalar", lambda e: e.activation(C_GS[:], C_UP[:], AF.Silu), "sE")
                p.wait("vector", "sE")
                p.run("vector", lambda e, ms=ms: e.scalar_tensor_tensor(
                    C_UP[:], psB[:, ms * 512:(ms + 1) * 512],
                    mw_t[:, 1:2], C_BC[:, 0:SC], ALU.mult, ALU.mult), "vE")
                evtok = p.run("vector", lambda e, gi=gi, ms=ms: e.tensor_tensor(
                    C_GU[:, gi * 2048 + ms * 512:gi * 2048 + (ms + 1) * 512],
                    C_GS[:], C_UP[:], ALU.mult), "vE")
        p.join()
        tap_out("gu" if sc == 0 else "_gu%d" % sc, C_GU[:], [128, 16384])

        slots = [1024, 1536]
        vtok = 0
        for blk in range(SC // 128):
            for fc in range(FC):
                i = blk * FC + fc
                sl = slots[i % 2]
                if i >= 2:
                    p.wait("tensor", "vE", vtok)
                p.run("tensor", lambda e, sl=sl, fc=fc, blk=blk: e.transpose(
                    psB[:, sl:sl + 128],
                    C_GU[:, fc * SC + blk * 128:fc * SC + (blk + 1) * 128],
                    ident[:]), "tE")
                p.wait("vector", "tE")
                vtok = p.run("vector", lambda e, sl=sl, fc=fc: e.reduce_max(
                    MAXC_C[:, 16 + fc:17 + fc], psB[:, sl:sl + 128],
                    axis=AXX, apply_absolute_value=True), "vE")
            p.run("vector", lambda e, blk=blk: e.reduce_max(
                MAXC_C[:, blk:blk + 1], MAXC_C[:, 16:48], axis=AXX), "vE")
        p.join(["tensor"])
        for blk in range(SC // 128):
            sl = slots[blk % 2]
            if blk >= 2:
                p.wait("tensor", "vE")
            p.run("tensor", lambda e, sl=sl, blk=blk: e.transpose(
                psB[0:1, sl:sl + 128], MAXC_C[:, blk:blk + 1], ident[:]), "tE")
            p.wait("vector", "tE")
            p.run("vector", lambda e, sl=sl, blk=blk: e.tensor_copy(
                R[3][0:1, blk * 128:(blk + 1) * 128], psB[0:1, sl:sl + 128]), "vE")
        p.run("vector", lambda e: e.tensor_scalar(
            R[3][0:1, 0:SC], R[3][0:1, 0:SC], 1e-5, None, ALU.max), "vE")
        p.run("vector", lambda e: e.tensor_scalar(
            R[1][0:1, 0:SC], R[3][0:1, 0:SC], float(np.float32(1.0 / 127.0)),
            None, ALU.mult), "vE")
        p.run("vector", lambda e: e.tensor_copy(RR[1][0:1, 0:SC], R[1][0:1, 0:SC]), "vE")
        p.run("vector", lambda e: e.reciprocal(R[1][0:1, 0:SC], R[3][0:1, 0:SC]), "vE")
        p.run("vector", lambda e: e.tensor_scalar(
            R[1][0:1, 0:SC], R[1][0:1, 0:SC], 127.0, None, ALU.mult), "vE")
        p.run("vector", lambda e: e.tensor_copy(RR[2][0:1, 0:SC], R[1][0:1, 0:SC]), "vE")
        p.join(["tensor"])
        emit_bcast(psA, 0, RR[2], SC)
        emit_bcast(psA, 512, RR[1], SC)
        p.wait("vector", "tE")
        p.run("vector", lambda e: e.tensor_copy(C_BC[:, 512:1024], psA[:, 0:512]), "vE")
        p.run("vector", lambda e: e.tensor_copy(C_BC[:, 1024:1536], psA[:, 512:1024]), "vE")
        for fc in range(FC):
            p.run("vector", lambda e, fc=fc: e.tensor_tensor(
                C_UP[:], C_GU[:, fc * SC:(fc + 1) * SC], C_BC[:, 512:1024],
                ALU.mult), "vE")
            p.run("vector", lambda e, fc=fc: e.tensor_scalar(
                C_XQ2[:, fc * SC:(fc + 1) * SC], C_UP[:],
                CBIG, CBIG, ALU.add, ALU.subtract), "vE")
        p.join()

        wdma = {(0, 0): p.dma("sync", "sq", C_WD[0][:], s_fd[0][:, 0:8192]),
                (0, 1): p.dma("sync", "sq", C_WD[1][:], s_fd[0][:, 8192:16384])}
        for dg in range(2):
            ps = psA if dg == 0 else psB
            tok = None
            for kh in range(2):
                p.wait("tensor", "sq", wdma[(dg, kh)])
                for k in range(16):
                    gk = kh * 16 + k
                    for ms in range(4):
                        tok = p.run("tensor", lambda e, ps=ps, k=k, ms=ms, kh=kh, gk=gk: e.matmul(
                            ps[:, ms * 512:(ms + 1) * 512],
                            C_WD[kh][:, k * 512 + ms * 128:k * 512 + (ms + 1) * 128],
                            C_XQ2[:, gk * SC:(gk + 1) * SC],
                            start=(gk == 0), stop=(gk == FC - 1)), "tE")
            if dg == 0:
                p.wait("sync", "tE", tok)
                wdma[(1, 0)] = p.dma("sync", "sq", C_WD[0][:], s_fd[1][:, 0:8192])
                wdma[(1, 1)] = p.dma("sync", "sq", C_WD[1][:], s_fd[1][:, 8192:16384])
            p.wait("vector", "tE", tok)
            for ms in range(4):
                c = dg * 4 + ms
                p.run("vector", lambda e, ps=ps, ms=ms: e.scalar_tensor_tensor(
                    C_TMP[:], ps[:, ms * 512:(ms + 1) * 512], mw_t[:, 2:3],
                    C_BC[:, 1024:1536], ALU.mult, ALU.mult), "vE")
                p.run("vector", lambda e, c=c: e.tensor_tensor(
                    C_X2[:, c * SC:(c + 1) * SC], C_X2[:, c * SC:(c + 1) * SC],
                    C_TMP[:], ALU.add), "vE")
        p.join(["sync"])
        out_sl = out_h.rearrange("(c p) t -> p c t", p=128)[:, :, sc * SC:(sc + 1) * SC]
        p.dma("sync", "sq", out_sl, C_X2.rearrange("p (c t) -> p c t", c=DC))

    p.join()
    p.wait("sync", "sq", p.ct["sq"])

    with nc.Block() as block:
        p.replay(block)
    es.close()
    return nc


def _tile_w(wT, kc, mc, mw=128):
    A = wT.reshape(kc, 128, mc, mw).transpose(2, 1, 0, 3)
    return np.ascontiguousarray(A.reshape(mc, 128, kc * mw)).astype(np.float32)


def _col_pack(v, ncol):
    return np.ascontiguousarray(v.reshape(ncol, 128).T).astype(np.float32)


def kernel(**inputs):
    g = {k: np.asarray(v, dtype=np.float32) for k, v in inputs.items()}
    x = g["L0_attn_x"]

    shared = {
        "w_in": _tile_w(np.ascontiguousarray(g["L0_attn_in_w"].T), 8, 16),
        "w_gate": _tile_w(np.ascontiguousarray(g["L0_attn_gate_w"].T), 8, 16),
        "w_b": _tile_w(np.ascontiguousarray(g["L0_attn_b_w"].T), 16, 16),
        "w_c": _tile_w(np.ascontiguousarray(g["L0_attn_c_w"].T), 16, 16),
        "w_d": _tile_w(np.ascontiguousarray(g["L0_attn_d_w"].T), 16, 16),
        "w_out": _tile_w(np.ascontiguousarray(g["L0_attn_out_w"].T), 16, 8),
        "w_fg": _tile_w(np.ascontiguousarray(g["L0_ffn_w_gate"].T), 8, 8, 512),
        "w_fu": _tile_w(np.ascontiguousarray(g["L0_ffn_w_up"].T), 8, 8, 512),
        "w_fd": _tile_w(np.ascontiguousarray(g["L0_ffn_w_down"].T), 32, 2, 512),
        "ident_in": np.eye(128, dtype=np.float32),
    }
    bias = np.zeros((128, 112), np.float32)
    bias[:, 0:16] = _col_pack(g["L0_attn_in_b"], 16)
    bias[:, 16:32] = _col_pack(g["L0_attn_gate_b"], 16)
    bias[:, 32:48] = _col_pack(g["L0_attn_b_b"], 16)
    bias[:, 48:64] = _col_pack(g["L0_attn_c_b"], 16)
    bias[:, 64:80] = _col_pack(g["L0_attn_d_b"], 16)
    bias[:, 80:88] = _col_pack(g["L0_attn_out_b"], 8)
    bias[:, 88:96] = _col_pack(g["L0_attn_norm_w"], 8)
    bias[:, 96:112] = _col_pack(g["L0_attn_a_log"], 16)
    shared["biasp"] = bias

    in_maps = []
    for c in range(8):
        bc, h = c // 2, c % 2
        m = dict(shared)
        m["x_fm"] = np.ascontiguousarray(x[bc, h * T:(h + 1) * T, :].T)
        m["maskc"] = np.full((128, 1), float(h), np.float32)
        in_maps.append(m)

    nc = build_nc()
    res = run_bass_kernel_spmd(nc, in_maps, core_ids=list(range(8)))
    out = np.empty((4, 8192, 1024), np.float32)
    for c in range(8):
        bc, h = c // 2, c % 2
        out[bc, h * T:(h + 1) * T, :] = np.asarray(res.results[c]["out_fm"]).T
    return out


# version 06
# speedup vs baseline: 1.1016x; 1.1016x over previous
import numpy as np
from contextlib import ExitStack

import concourse.bass as bass
import concourse.mybir as mybir
from concourse.bass_utils import run_bass_kernel_spmd

F32 = mybir.dt.float32
F32R = mybir.dt.float32r
BF16 = mybir.dt.bfloat16
U32 = mybir.dt.uint32
AF = mybir.ActivationFunctionType
ALU = mybir.AluOpType
AXX = mybir.AxisListType.X

T = 4096
SA = 1024
NS = T // SA
SC = 512
NSC = T // SC
D, DI, DF = 1024, 2048, 4096
DC, IC, FC = 8, 16, 32
RCP1024 = float(np.float32(2.0 ** -10))
CBIG = float(np.float32(3 * 2 ** 22))
CLIP15 = 1.4999999
CCGROUPS = [[0, 1], [2, 3], [4, 5], [6, 7]]


class Prog:
    ENGINES = ("sync", "scalar", "vector", "gpsimd", "tensor")

    def __init__(self, nc, sems):
        self.nc = nc
        self.sems = sems
        self.ct = {k: 0 for k in sems}
        self.items = []

    def wait(self, eng, sem, val=None):
        if val is None:
            val = self.ct[sem]
        if val <= 0:
            return
        s = self.sems[sem]
        self.items.append((eng, lambda e, s=s, v=val: e.wait_ge(s, v)))

    def dma(self, eng, sem, out, in_):
        self.ct[sem] += 16
        s = self.sems[sem]
        self.items.append(
            (eng, lambda e, s=s, o=out, i=in_: e.dma_start(out=o, in_=i).then_inc(s, 16)))
        return self.ct[sem]

    SELF_SEM = {"vector": "vE", "scalar": "sE"}

    def run(self, eng, fn, sem=None):
        own = self.SELF_SEM.get(eng)
        if own is not None and self.ct[own] > 0:
            self.wait(eng, own, self.ct[own])
        if sem is None:
            self.items.append((eng, fn))
            return None
        self.ct[sem] += 1
        s = self.sems[sem]
        self.items.append((eng, lambda e, s=s, f=fn: f(e).then_inc(s, 1)))
        return self.ct[sem]

    def join(self, engines=None):
        snap = dict(self.ct)
        for eng in (engines or self.ENGINES):
            for sem, v in snap.items():
                if v > 0:
                    self.wait(eng, sem, v)

    def replay(self, block):
        per = {e: [fn for (ee, fn) in self.items if ee == e] for e in self.ENGINES}

        @block.sync
        def _(e):
            for fn in per["sync"]:
                fn(e)

        @block.scalar
        def _(e):
            for fn in per["scalar"]:
                fn(e)

        @block.vector
        def _(e):
            for fn in per["vector"]:
                fn(e)

        @block.gpsimd
        def _(e):
            for fn in per["gpsimd"]:
                fn(e)

        @block.tensor
        def _(e):
            for fn in per["tensor"]:
                fn(e)


def build_nc(tap=None):
    nc = bass.Bass("TRN2", target_bir_lowering=False, detect_race_conditions=False)

    def din(name, shape):
        return nc.dram_tensor(name, shape, F32, kind="ExternalInput").ap()

    x_h = din("x_fm", [D, T])
    w_in_h = din("w_in", [16, 128, 8 * 128])
    w_gate_h = din("w_gate", [16, 128, 8 * 128])
    w_b_h = din("w_b", [16, 128, 16 * 128])
    w_c_h = din("w_c", [16, 128, 16 * 128])
    w_d_h = din("w_d", [16, 128, 16 * 128])
    w_out_h = din("w_out", [8, 128, 16 * 128])
    w_fg_h = din("w_fg", [8, 128, 8 * 512])
    w_fu_h = din("w_fu", [8, 128, 8 * 512])
    w_fd_h = din("w_fd", [2, 128, 32 * 512])
    bias_h = din("biasp", [128, 112])
    mask_h = din("maskc", [128, 1])
    ident_h = din("ident_in", [128, 128])
    out_h = nc.dram_tensor("out_fm", [D, T], F32, kind="ExternalOutput").ap()

    def scr(name, shape, dt):
        return nc.dram_tensor(name, shape, dt, kind="Internal").ap()

    s_in = scr("s_in", [16, 128, 8 * 128], BF16)
    s_gate = scr("s_gate", [16, 128, 8 * 128], BF16)
    s_b = scr("s_b", [16, 128, 16 * 128], BF16)
    s_c = scr("s_c", [16, 128, 16 * 128], BF16)
    s_d = scr("s_d", [16, 128, 16 * 128], BF16)
    s_out = scr("s_out", [8, 128, 16 * 128], BF16)
    s_fg = scr("s_fg", [8, 128, 8 * 512], BF16)
    s_fu = scr("s_fu", [8, 128, 8 * 512], BF16)
    s_fd = scr("s_fd", [2, 128, 32 * 512], BF16)
    u_sp = scr("u_sp", [NS, 128, IC * SA], BF16)
    b_sp = scr("b_sp", [NS, IC, 128, SA], F32)
    x2_sp = scr("x2_sp", [NS, 128, DC * SA], F32)
    xq1_sp = scr("xq1_sp", [NS, 128, DC * SA], BF16)
    inv1_sp = scr("inv1_sp", [NS, SA], F32)
    cc_in = scr("cc_in", [128, 16], F32)
    cc_out = scr("cc_out", [256, 16], F32)

    AR_N = 44032
    arena = nc.alloc_sbuf_tensor("arena", [128, AR_N], F32).ap()

    def reg(off, n, dt=F32):
        assert off + n <= AR_N, (off, n)
        ap = arena[:, off:off + n]
        if dt == BF16:
            return ap.bitcast(BF16)
        return ap

    X_T = reg(0, 8192)
    XQ1_T = reg(0, 4096, BF16)
    U_T = reg(8192, 8192, BF16)
    XB_T = reg(8192, 4096, BF16)
    XSQ_T = reg(12288, 4096, BF16)
    XN_T = reg(16384, 4096, BF16)
    Y_T = reg(20480, 8192, BF16)
    YQF_T = reg(20480, 8192)
    X2_T = reg(28672, 8192)
    WS = [reg(36864 + i * 1024, 1024, BF16) for i in range(4)]
    BW = [reg(40960 + i * 1024, 1024) for i in range(2)]
    SCO_A = reg(20480, 1024)
    TMPA_A = [reg(21504, 1024), reg(22528, 1024)]
    SCO_B = reg(16384, 1024)
    TMPA_B = [reg(17408, 1024), reg(18432, 1024)]
    TMP3_B = reg(19456, 1024)
    MAXC = reg(36864, 64)
    PT_F = [reg(0, 4096), reg(4096, 4096)]
    PT_B = [reg(8192, 2048, BF16), reg(10240, 2048, BF16)]
    C_WF = [reg(i * 2048, 2048, BF16) for i in range(4)]
    C_XQ1 = reg(8192, 2048, BF16)
    C_GU = reg(10240, 16384)
    C_WD = [reg(10240, 4096, BF16),
            reg(14336, 4096, BF16)]
    C_XQ2 = reg(26624, 8192, BF16)
    C_X2 = reg(34816, 4096)
    C_BC = reg(38912, 1536)
    C_TMP = reg(40448, 512)
    C_GS = reg(40960, 512)
    C_UP = reg(41472, 512)
    MAXC_C = reg(41984, 64)

    rows = nc.alloc_sbuf_tensor("rows", [1, 4 * 1024], F32).ap()
    R = [rows[0:1, i * 1024:(i + 1) * 1024] for i in range(4)]
    rows_r = nc.alloc_sbuf_tensor("rows_r", [1, 3 * 1024], F32R).ap()
    RR = [rows_r[0:1, i * 1024:(i + 1) * 1024] for i in range(3)]

    bias_t = nc.alloc_sbuf_tensor("bias_t", [128, 112], F32).ap()
    IN_B, GATE_B = bias_t[:, 0:16], bias_t[:, 16:32]
    B_B, C_B, D_B = bias_t[:, 32:48], bias_t[:, 48:64], bias_t[:, 64:80]
    OUT_B, NW, ALOG = bias_t[:, 80:88], bias_t[:, 88:96], bias_t[:, 96:112]
    nnw_t = nc.alloc_sbuf_tensor("nnw", [128, 8], F32).ap()
    a_t = nc.alloc_sbuf_tensor("a_t", [128, 16], F32).ap()
    a_tmp = nc.alloc_sbuf_tensor("a_tmp", [128, 64], F32).ap()
    a_msk = nc.alloc_sbuf_tensor("a_msk", [128, 16], U32).ap()
    state_t = nc.alloc_sbuf_tensor("state", [128, 16], F32).ap()
    hprev_t = nc.alloc_sbuf_tensor("hprev", [128, 16], F32).ap()
    mask_t = nc.alloc_sbuf_tensor("maskt", [128, 1], F32).ap()
    ones_bf = nc.alloc_sbuf_tensor("ones_bf", [128, 2], BF16).ap()
    ones_r = nc.alloc_sbuf_tensor("ones_r", [1, 128], F32R).ap()
    ones_f = nc.alloc_sbuf_tensor("ones_f", [1, 128], F32).ap()
    onec_f = nc.alloc_sbuf_tensor("onec_f", [128, 2], F32).ap()
    ident = nc.alloc_sbuf_tensor("ident", [128, 128], F32).ap()
    mw_t = nc.alloc_sbuf_tensor("mw_t", [128, 8], F32).ap()
    wsc_t = nc.alloc_sbuf_tensor("wsc_t", [1, 8], F32).ap()
    acc_t = nc.alloc_sbuf_tensor("acc_t", [128, 8], F32).ap()
    inv1_row = nc.alloc_sbuf_tensor("inv1_row", [1, 512], F32).ap()

    psA = nc.alloc_psum_tensor("psA", [128, 2048], F32).ap()
    psB = nc.alloc_psum_tensor("psB", [128, 2048], F32).ap()

    es = ExitStack()
    semnames = ["sq", "aq", "gq", "tE", "vE", "sE", "cc"]
    sems = {k: es.enter_context(nc.semaphore(k)) for k in semnames}
    p = Prog(nc, sems)
    tap_state = {"done": False}

    def tap_out(name, ap, shape, dt=F32):
        if tap != name or tap_state["done"]:
            return
        tap_state["done"] = True
        th = nc.dram_tensor("dbg", list(shape), dt, kind="ExternalOutput").ap()
        p.join()
        p.dma("sync", "sq", th[:], ap)
        p.join()

    def emit_rsqrt(src, dst, scratch):
        p.run("scalar", lambda e: e.activation(scratch, src, AF.Ln), "sE")
        p.run("scalar", lambda e: e.activation(dst, scratch, AF.Exp, scale=-0.5), "sE")
        p.join(["vector"])
        for _ in range(2):
            p.run("vector", lambda e: e.tensor_tensor(scratch, dst, dst, ALU.mult), "vE")
            p.run("vector", lambda e: e.tensor_tensor(scratch, scratch, src, ALU.mult), "vE")
            p.run("vector", lambda e: e.tensor_scalar(
                scratch, scratch, -0.5, 1.5, ALU.mult, ALU.add), "vE")
            p.run("vector", lambda e: e.tensor_tensor(dst, dst, scratch, ALU.mult), "vE")

    def emit_stats(xb, xsq, var_mult, var_eps, n):
        for src, base in ((xb, 0), (xsq, 1024)):
            for k in range(DC):
                for h in range(n // 512):
                    p.run("tensor", lambda e, s=src, k=k, h=h, b=base: e.matmul(
                        psA[0:1, b + h * 512:b + (h + 1) * 512],
                        ones_bf[:, 0:1],
                        s[:, k * n + h * 512:k * n + (h + 1) * 512],
                        start=(k == 0), stop=(k == DC - 1)), "tE")
        p.wait("vector", "tE")
        p.run("vector", lambda e: e.tensor_scalar(
            R[0][0:1, 0:n], psA[0:1, 0:n], RCP1024, None, ALU.mult), "vE")
        p.run("vector", lambda e: e.tensor_scalar(
            R[1][0:1, 0:n], psA[0:1, 1024:1024 + n], RCP1024, None, ALU.mult), "vE")
        p.run("vector", lambda e: e.tensor_tensor(
            R[3][0:1, 0:n], R[0][0:1, 0:n], R[0][0:1, 0:n], ALU.mult), "vE")
        p.run("vector", lambda e: e.tensor_tensor(
            R[1][0:1, 0:n], R[1][0:1, 0:n], R[3][0:1, 0:n], ALU.subtract), "vE")
        p.run("vector", lambda e: e.tensor_scalar(
            R[1][0:1, 0:n], R[1][0:1, 0:n], var_mult, var_eps, ALU.mult, ALU.add), "vE")
        p.join(["scalar"])
        emit_rsqrt(R[1][0:1, 0:n], R[2][0:1, 0:n], R[3][0:1, 0:n])

    def emit_bcast(ps, pb_off, row_r, n):
        for h in range(n // 512):
            p.run("tensor", lambda e, h=h: e.matmul(
                ps[:, pb_off + h * 512:pb_off + (h + 1) * 512],
                ones_r[0:1, :], row_r[0:1, h * 512:(h + 1) * 512],
                start=True, stop=True), "tE")

    p.dma("sync", "sq", bias_t[:], bias_h[:])
    p.dma("sync", "sq", mask_t[:], mask_h[:])
    p.dma("sync", "sq", ident[:], ident_h[:])
    for (dst, src) in [(s_in, w_in_h), (s_gate, w_gate_h), (s_b, w_b_h),
                       (s_c, w_c_h), (s_d, w_d_h), (s_out, w_out_h)]:
        p.dma("gpsimd", "gq", dst[:], src[:])
    for mset in [ones_bf, ones_f, onec_f]:
        p.run("vector", lambda e, m=mset: e.memset(m[:], 1.0), "vE")
    p.run("vector", lambda e: e.tensor_copy(ones_r[:], ones_f[:]), "vE")
    p.run("vector", lambda e: e.memset(state_t[:], 0.0), "vE")
    p.run("vector", lambda e: e.memset(acc_t[:], 0.0), "vE")
    p.wait("vector", "sq", 16)
    p.run("vector", lambda e: e.tensor_scalar(nnw_t[:], NW, -1.0, None, ALU.mult), "vE")

    AT = [a_tmp[:, i * 16:(i + 1) * 16] for i in range(4)]
    p.join(["scalar"])
    p.run("scalar", lambda e: e.activation(AT[0], ALOG, AF.Abs), "sE")
    p.run("scalar", lambda e: e.activation(AT[0], AT[0], AF.Exp, scale=-1.0), "sE")
    p.run("scalar", lambda e: e.activation(AT[1], AT[0], AF.Ln, bias=1.0), "sE")
    p.join(["vector"])
    p.run("vector", lambda e: e.tensor_scalar(AT[2], AT[0], -0.25, 1.0 / 3.0, ALU.mult, ALU.add), "vE")
    p.run("vector", lambda e: e.tensor_tensor(AT[2], AT[2], AT[0], ALU.mult), "vE")
    p.run("vector", lambda e: e.tensor_scalar(AT[2], AT[2], -0.5, None, ALU.add), "vE")
    p.run("vector", lambda e: e.tensor_tensor(AT[2], AT[2], AT[0], ALU.mult), "vE")
    p.run("vector", lambda e: e.tensor_scalar(AT[2], AT[2], 1.0, None, ALU.add), "vE")
    p.run("vector", lambda e: e.tensor_tensor(AT[2], AT[2], AT[0], ALU.mult), "vE")
    p.run("vector", lambda e: e.tensor_scalar(a_msk[:], AT[0], 0.01, None, ALU.is_lt), "vE")
    p.run("vector", lambda e: e.select(AT[3], a_msk[:], AT[2], AT[1]), "vE")
    p.run("vector", lambda e: e.tensor_scalar(AT[0], ALOG, 0.0, None, ALU.max), "vE")
    p.run("vector", lambda e: e.tensor_tensor(AT[3], AT[3], AT[0], ALU.add), "vE")
    p.join(["scalar"])
    p.run("scalar", lambda e: e.activation(a_t[:], AT[3], AF.Exp, scale=-1.0), "sE")
    tap_out("a", a_t[:], [128, 16])

    ffn_tiles = []
    for (src, dst, mi) in [(w_fg_h, s_fg, 0), (w_fu_h, s_fu, 1), (w_fd_h, s_fd, 2)]:
        if mi < 2:
            for gi in range(8):
                ffn_tiles.append((src[gi], dst[gi], mi))
        else:
            for gi in range(2):
                for q in range(4):
                    ffn_tiles.append((src[gi][:, q * 4096:(q + 1) * 4096],
                                      dst[gi][:, q * 4096:(q + 1) * 4096], mi))
    prev_v = 0
    for i, (sap, dap, mi) in enumerate(ffn_tiles):
        par = i % 2
        if i >= 2:
            p.wait("sync", "vE", prev_v)
        lt = p.dma("sync", "sq", PT_F[par][:], sap)
        p.wait("vector", "sq", lt)
        p.run("vector", lambda e, t=PT_F[par]: e.reduce_sum(
            acc_t[:, 6:7], t[:], axis=AXX, apply_absolute_value=True), "vE")
        prev_v = p.run("vector", lambda e, m=mi: e.tensor_tensor(
            acc_t[:, m:m + 1], acc_t[:, m:m + 1], acc_t[:, 6:7], ALU.add), "vE")
    p.join(["tensor"])
    for mi in range(3):
        p.run("tensor", lambda e, m=mi: e.matmul(
            psA[0:1, m * 512:m * 512 + 1], onec_f[:, 0:1], acc_t[:, m:m + 1],
            start=True, stop=True), "tE")
    p.wait("vector", "tE")
    for mi in range(3):
        p.run("vector", lambda e, m=mi: e.tensor_scalar(
            wsc_t[0:1, m:m + 1], psA[0:1, m * 512:m * 512 + 1],
            float(np.float32(2.0 ** -22)), 1e-5, ALU.mult, ALU.max), "vE")
    p.join(["tensor"])
    for mi in range(3):
        p.run("tensor", lambda e, m=mi: e.matmul(
            psB[:, m * 512:m * 512 + 1], ones_f[0:1, :], wsc_t[0:1, m:m + 1],
            start=True, stop=True), "tE")
    p.wait("vector", "tE")
    for mi in range(3):
        p.run("vector", lambda e, m=mi: e.tensor_copy(
            mw_t[:, m:m + 1], psB[:, m * 512:m * 512 + 1]), "vE")
    for mi in range(3):
        p.run("vector", lambda e, m=mi: e.reciprocal(
            mw_t[:, 3 + m:4 + m], mw_t[:, m:m + 1]), "vE")
    p.join()

    prev_v = 0
    prev_g = 0
    for i, (sap, dap, mi) in enumerate(ffn_tiles):
        par = i % 2
        if i >= 2:
            p.wait("sync", "vE", prev_v)
        lt = p.dma("sync", "sq", PT_F[par][:], sap)
        p.wait("vector", "sq", lt)
        if i >= 2:
            p.wait("vector", "gq", prev_g)
        ws = mw_t[:, 3 + mi:4 + mi]
        p.run("vector", lambda e, t=PT_F[par], w=ws: e.tensor_scalar(
            t[:], t[:], w, CLIP15, ALU.mult, ALU.min), "vE")
        p.run("vector", lambda e, t=PT_F[par]: e.tensor_scalar(
            t[:], t[:], -CLIP15, CBIG, ALU.max, ALU.add), "vE")
        prev_v = p.run("vector", lambda e, t=PT_F[par], o=PT_B[par]: e.tensor_scalar(
            o[:], t[:], CBIG, None, ALU.subtract), "vE")
        p.wait("gpsimd", "vE", prev_v)
        prev_g = p.dma("gpsimd", "gq", dap, PT_B[par][:])
    p.join()

    x_r = x_h.rearrange("(c p) t -> p c t", p=128)
    X3 = X_T.rearrange("p (c t) -> p c t", c=DC)
    XB3 = XB_T.rearrange("p (c t) -> p c t", c=DC)

    def emit_ln1_head(s, need_xn, TMPA):
        xsl = x_r[:, :, s * SA:(s + 1) * SA]
        p.dma("sync", "sq", X3[:, :, :], xsl)
        p.dma("gpsimd", "gq", XB3[:, :, :], xsl)
        p.wait("scalar", "sq")
        p.wait("scalar", "gq")
        p.run("scalar", lambda e: e.activation(XSQ_T[:], X_T[:], AF.Square), "sE")
        p.wait("tensor", "gq")
        p.wait("tensor", "sE")
        if need_xn:
            emit_stats(XB_T, XSQ_T, 1.000001, 1e-11, SA)
            p.run("vector", lambda e: e.tensor_copy(RR[0][0:1, 0:SA], R[2][0:1, 0:SA]), "vE")
            p.run("vector", lambda e: e.tensor_tensor(
                R[3][0:1, 0:SA], R[0][0:1, 0:SA], R[2][0:1, 0:SA], ALU.mult), "vE")
            p.run("vector", lambda e: e.tensor_copy(RR[1][0:1, 0:SA], R[3][0:1, 0:SA]), "vE")
            p.join(["tensor"])
            emit_bcast(psB, 0, RR[0], SA)
            emit_bcast(psB, 1024, RR[1], SA)
            p.wait("vector", "tE")
            for c in range(DC):
                p.run("vector", lambda e, c=c: e.scalar_tensor_tensor(
                    TMPA[0][:], X_T[:, c * SA:(c + 1) * SA], NW[:, c:c + 1],
                    psB[:, 0:SA], ALU.mult, ALU.mult), "vE")
                p.run("vector", lambda e, c=c: e.scalar_tensor_tensor(
                    XN_T[:, c * SA:(c + 1) * SA], psB[:, 1024:1024 + SA],
                    nnw_t[:, c:c + 1], TMPA[0][:], ALU.mult, ALU.add), "vE")
        else:
            emit_stats(XB_T, XSQ_T, 1.0, 1e-5, SA)
            p.run("vector", lambda e: e.tensor_tensor(
                R[3][0:1, 0:SA], R[0][0:1, 0:SA], R[2][0:1, 0:SA], ALU.mult), "vE")
            p.run("vector", lambda e: e.tensor_copy(RR[1][0:1, 0:SA], R[3][0:1, 0:SA]), "vE")
            p.run("vector", lambda e: e.tensor_scalar(
                R[3][0:1, 0:SA], R[2][0:1, 0:SA], 1.0, None, ALU.add), "vE")
            p.run("vector", lambda e: e.tensor_copy(RR[2][0:1, 0:SA], R[3][0:1, 0:SA]), "vE")

    def emit_mm_loop(nm, kc, wsrc_fn, rhs, n, evac_fn):
        dmatok = [0] * nm
        mmtok = [0] * nm
        etok = [None] * nm
        for m in range(min(2, nm)):
            dmatok[m] = p.dma("sync", "sq", WS[m % 2][:, 0:kc * 128], wsrc_fn(m))
        for m in range(nm):
            par = m % 2
            ps = psA if par == 0 else psB
            p.wait("tensor", "sq", dmatok[m])
            if m >= 2 and etok[m - 2] is not None:
                p.wait("tensor", etok[m - 2][0], etok[m - 2][1])
            tok = None
            for k in range(kc):
                for h in range(n // 512):
                    tok = p.run("tensor", lambda e, ps=ps, k=k, h=h, par=par: e.matmul(
                        ps[:, h * 512:(h + 1) * 512],
                        WS[par][:, k * 128:(k + 1) * 128],
                        rhs[:, k * n + h * 512:k * n + (h + 1) * 512],
                        start=(k == 0), stop=(k == kc - 1)), "tE")
            mmtok[m] = tok
            if m + 2 < nm:
                p.wait("sync", "tE", mmtok[m])
                dmatok[m + 2] = p.dma("sync", "sq", WS[par][:, 0:kc * 128], wsrc_fn(m + 2))
            etok[m] = evac_fn(m, ps)
        return mmtok

    def emit_mm_loop2(nm, kc, wsrc1, wsrc2, rhs, n, evac_fn, pre_fn=None):
        dmatok = [[0] * nm, [0] * nm]
        mmtok = [0] * nm
        etok = [None] * nm
        for m in range(min(2, nm)):
            dmatok[0][m] = p.dma("sync", "sq", WS[m % 2][:, 0:kc * 128], wsrc1(m))
            dmatok[1][m] = p.dma("sync", "sq", WS[2 + m % 2][:, 0:kc * 128], wsrc2(m))
        for m in range(nm):
            par = m % 2
            ps = psA if par == 0 else psB
            if pre_fn is not None:
                pre_fn(m)
            p.wait("tensor", "sq", max(dmatok[0][m], dmatok[1][m]))
            if m >= 2 and etok[m - 2] is not None:
                p.wait("tensor", etok[m - 2][0], etok[m - 2][1])
            tok = None
            for wi, wb in ((0, WS[par]), (1, WS[2 + par])):
                for k in range(kc):
                    for h in range(n // 512):
                        tok = p.run("tensor", lambda e, ps=ps, k=k, h=h, wb=wb, wi=wi: e.matmul(
                            ps[:, wi * 1024 + h * 512:wi * 1024 + (h + 1) * 512],
                            wb[:, k * 128:(k + 1) * 128],
                            rhs[:, k * n + h * 512:k * n + (h + 1) * 512],
                            start=(k == 0), stop=(k == kc - 1)), "tE")
            mmtok[m] = tok
            if m + 2 < nm:
                p.wait("sync", "tE", mmtok[m])
                dmatok[0][m + 2] = p.dma("sync", "sq", WS[par][:, 0:kc * 128], wsrc1(m + 2))
                dmatok[1][m + 2] = p.dma("sync", "sq", WS[2 + par][:, 0:kc * 128], wsrc2(m + 2))
            etok[m] = evac_fn(m, ps)
        return mmtok

    for s in range(NS):
        p.join()
        emit_ln1_head(s, True, TMPA_A)
        tap_out("xn" if s == 0 else "_xn%d" % s, XN_T[:], [128, 8192], BF16)
        p.join()

        def evac_ug(m, ps):
            par = m % 2
            p.wait("scalar", "tE")
            p.run("scalar", lambda e, ps=ps, m=m, par=par: e.activation(
                TMPA_A[par][:], ps[:, 0:SA], AF.Identity, bias=IN_B[:, m:m + 1]), "sE")
            tok = p.run("scalar", lambda e, ps=ps, m=m, par=par: e.activation(
                BW[par][:], ps[:, 1024:1024 + SA], AF.Sigmoid, bias=GATE_B[:, m:m + 1]), "sE")
            p.wait("vector", "sE")
            p.run("vector", lambda e, m=m, par=par: e.tensor_tensor(
                U_T[:, m * SA:(m + 1) * SA], TMPA_A[par][:], BW[par][:], ALU.mult), "vE")
            return ("sE", tok)

        emit_mm_loop2(IC, DC, lambda m: s_in[m], lambda m: s_gate[m],
                      XN_T, SA, evac_ug)
        p.join()
        tap_out("u" if s == 0 else "_u%d" % s, U_T[:], [128, 16384], BF16)
        p.dma("gpsimd", "gq", u_sp[s], U_T[:])

        def evac_b(m, ps, s=s):
            par = m % 2
            if m >= 2:
                p.wait("scalar", "vE")
                p.wait("scalar", "gq")
            p.wait("scalar", "tE")
            tok = p.run("scalar", lambda e, ps=ps, m=m, par=par: e.activation(
                BW[par][:], ps[:, 0:SA], AF.Identity, bias=B_B[:, m:m + 1]), "sE")
            p.wait("gpsimd", "sE")
            p.dma("gpsimd", "gq", b_sp[s, m], BW[par][:])
            p.wait("vector", "sE")
            p.run("vector", lambda e, m=m, par=par: e.tensor_tensor_scan(
                SCO_A[:], a_t[:, m:m + 1].broadcast_to((128, SA)), BW[par][:],
                state_t[:, m:m + 1], ALU.mult, ALU.add), "vE")
            p.run("vector", lambda e, m=m: e.tensor_copy(
                state_t[:, m:m + 1], SCO_A[:, SA - 1:SA]), "vE")
            return ("sE", tok)

        emit_mm_loop(IC, IC, lambda m: s_b[m], U_T, SA, evac_b)
        p.join()
    tap_out("hend", state_t[:], [128, 16])

    p.join()
    p.dma("gpsimd", "gq", cc_in[:], state_t[:])
    p.wait("gpsimd", "gq")
    p.run("gpsimd", lambda e: e.collective_compute(
        "AllGather", ALU.bypass,
        replica_groups=CCGROUPS,
        ins=[cc_in[:]], outs=[cc_out[:]]), "cc")
    p.wait("gpsimd", "cc")
    p.dma("gpsimd", "gq", hprev_t[:], cc_out[0:128, :])
    p.wait("vector", "gq")
    p.run("vector", lambda e: e.tensor_scalar(
        state_t[:], hprev_t[:], mask_t[:, 0:1], None, ALU.mult), "vE")
    p.join()
    tap_out("hseed", state_t[:], [128, 16])

    for s in range(NS):
        p.join()
        emit_ln1_head(s, False, TMPA_B)
        p.join()
        p.dma("sync", "sq", U_T[:], u_sp[s])

        btok = {}

        def pre_cd(m, s=s, btok=btok):
            if m == 0:
                btok[0] = p.dma("sync", "aq", BW[0][:], b_sp[s, 0])
            if m + 1 < IC:
                p.wait("sync", "vE")
                btok[m + 1] = p.dma("sync", "aq", BW[(m + 1) % 2][:], b_sp[s, m + 1])
            p.wait("vector", "aq", btok[m])
            p.run("vector", lambda e, m=m: e.tensor_tensor_scan(
                SCO_B[:], a_t[:, m:m + 1].broadcast_to((128, SA)), BW[m % 2][:],
                state_t[:, m:m + 1], ALU.mult, ALU.add), "vE")
            p.run("vector", lambda e, m=m: e.tensor_copy(
                state_t[:, m:m + 1], SCO_B[:, SA - 1:SA]), "vE")

        def evac_cd(m, ps):
            par = m % 2
            p.wait("scalar", "tE")
            p.run("scalar", lambda e, ps=ps, m=m, par=par: e.activation(
                TMPA_B[par][:], ps[:, 0:SA], AF.Identity, bias=C_B[:, m:m + 1]), "sE")
            tok = p.run("scalar", lambda e, ps=ps, m=m: e.activation(
                TMP3_B[:], ps[:, 1024:1024 + SA], AF.Identity, bias=D_B[:, m:m + 1]), "sE")
            p.wait("vector", "sE")
            p.run("vector", lambda e, par=par: e.tensor_tensor(
                TMPA_B[par][:], TMPA_B[par][:], SCO_B[:], ALU.mult), "vE")
            p.run("vector", lambda e, m=m, par=par: e.tensor_tensor(
                Y_T[:, m * SA:(m + 1) * SA], TMPA_B[par][:], TMP3_B[:], ALU.add), "vE")
            return ("sE", tok)

        p.wait("tensor", "sq")
        emit_mm_loop2(IC, IC, lambda m: s_c[m], lambda m: s_d[m],
                      U_T, SA, evac_cd, pre_fn=pre_cd)
        p.join()
        tap_out("y" if s == 0 else "_y%d" % s, Y_T[:], [128, 16384], BF16)

        def evac_out(m, ps):
            p.wait("scalar", "tE")
            tok = p.run("scalar", lambda e, ps=ps, m=m: e.activation(
                X2_T[:, m * SA:(m + 1) * SA], ps[:, 0:SA], AF.Identity,
                bias=OUT_B[:, m:m + 1]), "sE")
            return ("sE", tok)

        emit_mm_loop(DC, IC, lambda m: s_out[m], Y_T, SA, evac_out)
        p.join()

        emit_bcast(psB, 0, RR[2], SA)
        emit_bcast(psB, 1024, RR[1], SA)
        p.wait("vector", "tE")
        for c in range(DC):
            p.run("vector", lambda e, c=c: e.tensor_tensor(
                TMPA_B[0][:], X_T[:, c * SA:(c + 1) * SA], psB[:, 0:SA], ALU.mult), "vE")
            p.run("vector", lambda e, c=c: e.scalar_tensor_tensor(
                TMPA_B[0][:], psB[:, 1024:1024 + SA], -1.0, TMPA_B[0][:],
                ALU.mult, ALU.add), "vE")
            p.run("vector", lambda e, c=c: e.tensor_tensor(
                X2_T[:, c * SA:(c + 1) * SA], X2_T[:, c * SA:(c + 1) * SA],
                TMPA_B[0][:], ALU.add), "vE")
        p.join()
        tap_out("x2" if s == 0 else "_x2%d" % s, X2_T[:], [128, 8192])
        p.dma("gpsimd", "gq", x2_sp[s], X2_T[:])

        p.run("vector", lambda e: e.tensor_copy(XB_T[:], X2_T[:]), "vE")
        p.run("scalar", lambda e: e.activation(XSQ_T[:], X2_T[:], AF.Square), "sE")
        p.join(["tensor"])
        emit_stats(XB_T, XSQ_T, 1.0, 1e-5, SA)
        p.run("vector", lambda e: e.tensor_copy(RR[0][0:1, 0:SA], R[0][0:1, 0:SA]), "vE")
        p.join(["tensor"])
        emit_bcast(psB, 0, RR[0], SA)
        p.wait("vector", "tE")
        for c in range(DC):
            p.run("vector", lambda e, c=c: e.scalar_tensor_tensor(
                X_T[:, c * SA:(c + 1) * SA], psB[:, 0:SA], -1.0,
                X2_T[:, c * SA:(c + 1) * SA], ALU.mult, ALU.add), "vE")
        p.join(["tensor"])
        slots = [(psA, 0), (psA, 1024), (psB, 0), (psB, 1024)]
        vtoks = {}
        for blk in range(SA // 128):
            for c in range(DC):
                i = blk * DC + c
                ps_s, sl = slots[i % 4]
                if i >= 4:
                    p.wait("tensor", "vE", vtoks[i - 4])
                p.run("tensor", lambda e, ps_s=ps_s, sl=sl, c=c, blk=blk: e.transpose(
                    ps_s[:, sl:sl + 128],
                    X_T[:, c * SA + blk * 128:c * SA + (blk + 1) * 128],
                    ident[:]), "tE")
                p.wait("vector", "tE")
                vtoks[i] = p.run("vector", lambda e, ps_s=ps_s, sl=sl, c=c: e.reduce_max(
                    MAXC[:, 8 + c:9 + c], ps_s[:, sl:sl + 128], axis=AXX,
                    apply_absolute_value=True), "vE")
            p.run("vector", lambda e, blk=blk: e.reduce_max(
                MAXC[:, blk:blk + 1], MAXC[:, 8:16], axis=AXX), "vE")
        p.join(["tensor"])
        for blk in range(SA // 128):
            ps_s, sl = slots[blk % 4]
            if blk >= 4:
                p.wait("tensor", "vE")
            p.run("tensor", lambda e, ps_s=ps_s, sl=sl, blk=blk: e.transpose(
                ps_s[0:1, sl:sl + 128], MAXC[:, blk:blk + 1], ident[:]), "tE")
            p.wait("vector", "tE")
            p.run("vector", lambda e, ps_s=ps_s, sl=sl, blk=blk: e.tensor_copy(
                R[3][0:1, blk * 128:(blk + 1) * 128], ps_s[0:1, sl:sl + 128]), "vE")
        p.run("vector", lambda e: e.tensor_tensor(
            R[3][0:1, 0:SA], R[3][0:1, 0:SA], R[2][0:1, 0:SA], ALU.mult), "vE")
        p.run("vector", lambda e: e.tensor_scalar(
            R[3][0:1, 0:SA], R[3][0:1, 0:SA], 1e-5, None, ALU.max), "vE")
        p.run("vector", lambda e: e.tensor_scalar(
            R[1][0:1, 0:SA], R[3][0:1, 0:SA], float(np.float32(1.0 / 127.0)),
            None, ALU.mult), "vE")
        p.wait("gpsimd", "vE")
        sp_tok = p.dma("gpsimd", "gq", inv1_sp[s:s + 1, :], R[1][0:1, 0:SA])
        p.wait("vector", "gq", sp_tok)
        p.run("vector", lambda e: e.reciprocal(R[1][0:1, 0:SA], R[3][0:1, 0:SA]), "vE")
        p.run("vector", lambda e: e.tensor_scalar(
            R[1][0:1, 0:SA], R[1][0:1, 0:SA], 127.0, None, ALU.mult), "vE")
        p.run("vector", lambda e: e.tensor_tensor(
            R[1][0:1, 0:SA], R[1][0:1, 0:SA], R[2][0:1, 0:SA], ALU.mult), "vE")
        p.run("vector", lambda e: e.tensor_copy(RR[1][0:1, 0:SA], R[1][0:1, 0:SA]), "vE")
        p.join(["tensor"])
        emit_bcast(psB, 0, RR[1], SA)
        p.wait("vector", "tE")
        for c in range(DC):
            p.run("vector", lambda e, c=c: e.tensor_tensor(
                YQF_T[:, c * SA:(c + 1) * SA], X_T[:, c * SA:(c + 1) * SA],
                psB[:, 0:SA], ALU.mult), "vE")
            p.run("vector", lambda e, c=c: e.tensor_scalar(
                XQ1_T[:, c * SA:(c + 1) * SA], YQF_T[:, c * SA:(c + 1) * SA],
                CBIG, CBIG, ALU.add, ALU.subtract), "vE")
        p.join(["gpsimd"])
        p.dma("gpsimd", "gq", xq1_sp[s], XQ1_T[:])
        tap_out("xq1" if s == 0 else "_xq1%d" % s, XQ1_T[:], [128, 8192], BF16)

    for sc in range(NSC):
        s, t0 = sc // 2, (sc % 2) * SC
        p.join()
        xq1s = xq1_sp[s].rearrange("p (c t) -> p c t", c=DC)[:, :, t0:t0 + SC]
        x2s = x2_sp[s].rearrange("p (c t) -> p c t", c=DC)[:, :, t0:t0 + SC]
        p.dma("sync", "sq", C_XQ1.rearrange("p (c t) -> p c t", c=DC), xq1s)
        p.dma("sync", "sq", C_X2.rearrange("p (c t) -> p c t", c=DC), x2s)
        p.dma("sync", "aq", inv1_row[0:1, 0:SC], inv1_sp[s:s + 1, t0:t0 + SC])
        p.wait("vector", "aq")
        p.run("vector", lambda e: e.tensor_copy(RR[0][0:1, 0:SC], inv1_row[0:1, 0:SC]), "vE")
        p.join(["tensor"])
        emit_bcast(psA, 0, RR[0], SC)
        p.wait("vector", "tE")
        p.run("vector", lambda e: e.tensor_copy(C_BC[:, 0:SC], psA[:, 0:SC]), "vE")
        p.join()

        gdma = [[0] * 8, [0] * 8]
        for gi in range(2):
            gdma[0][gi] = p.dma("sync", "sq", C_WF[gi % 2][:], s_fg[gi])
            gdma[1][gi] = p.dma("sync", "aq", C_WF[2 + gi % 2][:], s_fu[gi])
        evtok = 0
        for gi in range(8):
            par = gi % 2
            if gi >= 1:
                p.wait("tensor", "vE", evtok)
            p.wait("tensor", "sq", gdma[0][gi])
            p.wait("tensor", "aq", gdma[1][gi])
            tok = None
            for wb, ps in ((C_WF[par], psA), (C_WF[2 + par], psB)):
                for k in range(DC):
                    for ms in range(4):
                        tok = p.run("tensor", lambda e, ps=ps, k=k, ms=ms, wb=wb: e.matmul(
                            ps[:, ms * 512:(ms + 1) * 512],
                            wb[:, k * 512 + ms * 128:k * 512 + (ms + 1) * 128],
                            C_XQ1[:, k * SC:(k + 1) * SC],
                            start=(k == 0), stop=(k == DC - 1)), "tE")
            if gi + 2 < 8:
                p.wait("sync", "tE", tok)
                p.wait("sync", "vE", evtok)
                gdma[0][gi + 2] = p.dma("sync", "sq", C_WF[par][:], s_fg[gi + 2])
                gdma[1][gi + 2] = p.dma("sync", "aq", C_WF[2 + par][:], s_fu[gi + 2])
            p.wait("vector", "tE", tok)
            for ms in range(4):
                p.run("vector", lambda e, ms=ms: e.scalar_tensor_tensor(
                    C_UP[:], psA[:, ms * 512:(ms + 1) * 512],
                    mw_t[:, 0:1], C_BC[:, 0:SC], ALU.mult, ALU.mult), "vE")
                p.wait("scalar", "vE")
                p.run("scalar", lambda e: e.activation(C_GS[:], C_UP[:], AF.Silu), "sE")
                p.wait("vector", "sE")
                p.run("vector", lambda e, ms=ms: e.scalar_tensor_tensor(
                    C_UP[:], psB[:, ms * 512:(ms + 1) * 512],
                    mw_t[:, 1:2], C_BC[:, 0:SC], ALU.mult, ALU.mult), "vE")
                evtok = p.run("vector", lambda e, gi=gi, ms=ms: e.tensor_tensor(
                    C_GU[:, gi * 2048 + ms * 512:gi * 2048 + (ms + 1) * 512],
                    C_GS[:], C_UP[:], ALU.mult), "vE")
        p.join()
        tap_out("gu" if sc == 0 else "_gu%d" % sc, C_GU[:], [128, 16384])

        slots = [(psA, 0), (psA, 1024), (psB, 0), (psB, 1024)]
        vtoks = {}
        for blk in range(SC // 128):
            for fc in range(FC):
                i = blk * FC + fc
                ps_s, sl = slots[i % 4]
                if i >= 4:
                    p.wait("tensor", "vE", vtoks[i - 4])
                p.run("tensor", lambda e, ps_s=ps_s, sl=sl, fc=fc, blk=blk: e.transpose(
                    ps_s[:, sl:sl + 128],
                    C_GU[:, fc * SC + blk * 128:fc * SC + (blk + 1) * 128],
                    ident[:]), "tE")
                p.wait("vector", "tE")
                vtoks[i] = p.run("vector", lambda e, ps_s=ps_s, sl=sl, fc=fc: e.reduce_max(
                    MAXC_C[:, 16 + fc:17 + fc], ps_s[:, sl:sl + 128],
                    axis=AXX, apply_absolute_value=True), "vE")
            p.run("vector", lambda e, blk=blk: e.reduce_max(
                MAXC_C[:, blk:blk + 1], MAXC_C[:, 16:48], axis=AXX), "vE")
        p.join(["tensor"])
        for blk in range(SC // 128):
            ps_s, sl = slots[blk % 4]
            if blk >= 4:
                p.wait("tensor", "vE")
            p.run("tensor", lambda e, ps_s=ps_s, sl=sl, blk=blk: e.transpose(
                ps_s[0:1, sl:sl + 128], MAXC_C[:, blk:blk + 1], ident[:]), "tE")
            p.wait("vector", "tE")
            p.run("vector", lambda e, ps_s=ps_s, sl=sl, blk=blk: e.tensor_copy(
                R[3][0:1, blk * 128:(blk + 1) * 128], ps_s[0:1, sl:sl + 128]), "vE")
        p.run("vector", lambda e: e.tensor_scalar(
            R[3][0:1, 0:SC], R[3][0:1, 0:SC], 1e-5, None, ALU.max), "vE")
        p.run("vector", lambda e: e.tensor_scalar(
            R[1][0:1, 0:SC], R[3][0:1, 0:SC], float(np.float32(1.0 / 127.0)),
            None, ALU.mult), "vE")
        p.run("vector", lambda e: e.tensor_copy(RR[1][0:1, 0:SC], R[1][0:1, 0:SC]), "vE")
        p.run("vector", lambda e: e.reciprocal(R[1][0:1, 0:SC], R[3][0:1, 0:SC]), "vE")
        p.run("vector", lambda e: e.tensor_scalar(
            R[1][0:1, 0:SC], R[1][0:1, 0:SC], 127.0, None, ALU.mult), "vE")
        p.run("vector", lambda e: e.tensor_copy(RR[2][0:1, 0:SC], R[1][0:1, 0:SC]), "vE")
        p.join(["tensor"])
        emit_bcast(psA, 0, RR[2], SC)
        emit_bcast(psA, 512, RR[1], SC)
        p.wait("vector", "tE")
        p.run("vector", lambda e: e.tensor_copy(C_BC[:, 512:1024], psA[:, 0:512]), "vE")
        p.run("vector", lambda e: e.tensor_copy(C_BC[:, 1024:1536], psA[:, 512:1024]), "vE")
        for fc in range(FC):
            p.run("vector", lambda e, fc=fc: e.tensor_tensor(
                C_UP[:], C_GU[:, fc * SC:(fc + 1) * SC], C_BC[:, 512:1024],
                ALU.mult), "vE")
            p.run("vector", lambda e, fc=fc: e.tensor_scalar(
                C_XQ2[:, fc * SC:(fc + 1) * SC], C_UP[:],
                CBIG, CBIG, ALU.add, ALU.subtract), "vE")
        p.join()

        wdma = {(0, 0): p.dma("sync", "sq", C_WD[0][:], s_fd[0][:, 0:8192]),
                (0, 1): p.dma("sync", "sq", C_WD[1][:], s_fd[0][:, 8192:16384])}
        for dg in range(2):
            ps = psA if dg == 0 else psB
            tok = None
            for kh in range(2):
                p.wait("tensor", "sq", wdma[(dg, kh)])
                for k in range(16):
                    gk = kh * 16 + k
                    for ms in range(4):
                        tok = p.run("tensor", lambda e, ps=ps, k=k, ms=ms, kh=kh, gk=gk: e.matmul(
                            ps[:, ms * 512:(ms + 1) * 512],
                            C_WD[kh][:, k * 512 + ms * 128:k * 512 + (ms + 1) * 128],
                            C_XQ2[:, gk * SC:(gk + 1) * SC],
                            start=(gk == 0), stop=(gk == FC - 1)), "tE")
            if dg == 0:
                p.wait("sync", "tE", tok)
                wdma[(1, 0)] = p.dma("sync", "sq", C_WD[0][:], s_fd[1][:, 0:8192])
                wdma[(1, 1)] = p.dma("sync", "sq", C_WD[1][:], s_fd[1][:, 8192:16384])
            p.wait("vector", "tE", tok)
            for ms in range(4):
                c = dg * 4 + ms
                p.run("vector", lambda e, ps=ps, ms=ms: e.scalar_tensor_tensor(
                    C_TMP[:], ps[:, ms * 512:(ms + 1) * 512], mw_t[:, 2:3],
                    C_BC[:, 1024:1536], ALU.mult, ALU.mult), "vE")
                p.run("vector", lambda e, c=c: e.tensor_tensor(
                    C_X2[:, c * SC:(c + 1) * SC], C_X2[:, c * SC:(c + 1) * SC],
                    C_TMP[:], ALU.add), "vE")
        p.join(["sync"])
        out_sl = out_h.rearrange("(c p) t -> p c t", p=128)[:, :, sc * SC:(sc + 1) * SC]
        p.dma("sync", "sq", out_sl, C_X2.rearrange("p (c t) -> p c t", c=DC))

    p.join()
    p.wait("sync", "sq", p.ct["sq"])

    with nc.Block() as block:
        p.replay(block)
    es.close()
    return nc


def _tile_w(wT, kc, mc, mw=128):
    A = wT.reshape(kc, 128, mc, mw).transpose(2, 1, 0, 3)
    return np.ascontiguousarray(A.reshape(mc, 128, kc * mw)).astype(np.float32)


def _col_pack(v, ncol):
    return np.ascontiguousarray(v.reshape(ncol, 128).T).astype(np.float32)


def kernel(**inputs):
    g = {k: np.asarray(v, dtype=np.float32) for k, v in inputs.items()}
    x = g["L0_attn_x"]

    shared = {
        "w_in": _tile_w(np.ascontiguousarray(g["L0_attn_in_w"].T), 8, 16),
        "w_gate": _tile_w(np.ascontiguousarray(g["L0_attn_gate_w"].T), 8, 16),
        "w_b": _tile_w(np.ascontiguousarray(g["L0_attn_b_w"].T), 16, 16),
        "w_c": _tile_w(np.ascontiguousarray(g["L0_attn_c_w"].T), 16, 16),
        "w_d": _tile_w(np.ascontiguousarray(g["L0_attn_d_w"].T), 16, 16),
        "w_out": _tile_w(np.ascontiguousarray(g["L0_attn_out_w"].T), 16, 8),
        "w_fg": _tile_w(np.ascontiguousarray(g["L0_ffn_w_gate"].T), 8, 8, 512),
        "w_fu": _tile_w(np.ascontiguousarray(g["L0_ffn_w_up"].T), 8, 8, 512),
        "w_fd": _tile_w(np.ascontiguousarray(g["L0_ffn_w_down"].T), 32, 2, 512),
        "ident_in": np.eye(128, dtype=np.float32),
    }
    bias = np.zeros((128, 112), np.float32)
    bias[:, 0:16] = _col_pack(g["L0_attn_in_b"], 16)
    bias[:, 16:32] = _col_pack(g["L0_attn_gate_b"], 16)
    bias[:, 32:48] = _col_pack(g["L0_attn_b_b"], 16)
    bias[:, 48:64] = _col_pack(g["L0_attn_c_b"], 16)
    bias[:, 64:80] = _col_pack(g["L0_attn_d_b"], 16)
    bias[:, 80:88] = _col_pack(g["L0_attn_out_b"], 8)
    bias[:, 88:96] = _col_pack(g["L0_attn_norm_w"], 8)
    bias[:, 96:112] = _col_pack(g["L0_attn_a_log"], 16)
    shared["biasp"] = bias

    in_maps = []
    for c in range(8):
        bc, h = c // 2, c % 2
        m = dict(shared)
        m["x_fm"] = np.ascontiguousarray(x[bc, h * T:(h + 1) * T, :].T)
        m["maskc"] = np.full((128, 1), float(h), np.float32)
        in_maps.append(m)

    nc = build_nc()
    res = run_bass_kernel_spmd(nc, in_maps, core_ids=list(range(8)))
    out = np.empty((4, 8192, 1024), np.float32)
    for c in range(8):
        bc, h = c // 2, c % 2
        out[bc, h * T:(h + 1) * T, :] = np.asarray(res.results[c]["out_fm"]).T
    return out


# version 07 (final)
# speedup vs baseline: 1.1388x; 1.0338x over previous
import numpy as np
from contextlib import ExitStack

import concourse.bass as bass
import concourse.mybir as mybir
from concourse.bass_utils import run_bass_kernel_spmd

F32 = mybir.dt.float32
F32R = mybir.dt.float32r
BF16 = mybir.dt.bfloat16
U32 = mybir.dt.uint32
AF = mybir.ActivationFunctionType
ALU = mybir.AluOpType
AXX = mybir.AxisListType.X

T = 4096
SA = 1024
NS = T // SA
SC = 512
NSC = T // SC
D, DI, DF = 1024, 2048, 4096
DC, IC, FC = 8, 16, 32
RCP1024 = float(np.float32(2.0 ** -10))
CBIG = float(np.float32(3 * 2 ** 22))
CLIP15 = 1.4999999
CCGROUPS = [[0, 1], [2, 3], [4, 5], [6, 7]]


class Prog:
    ENGINES = ("sync", "scalar", "vector", "gpsimd", "tensor")

    def __init__(self, nc, sems):
        self.nc = nc
        self.sems = sems
        self.ct = {k: 0 for k in sems}
        self.items = []

    def wait(self, eng, sem, val=None):
        if val is None:
            val = self.ct[sem]
        if val <= 0:
            return
        s = self.sems[sem]
        self.items.append((eng, lambda e, s=s, v=val: e.wait_ge(s, v)))

    def dma(self, eng, sem, out, in_):
        self.ct[sem] += 16
        s = self.sems[sem]
        self.items.append(
            (eng, lambda e, s=s, o=out, i=in_: e.dma_start(out=o, in_=i).then_inc(s, 16)))
        return self.ct[sem]

    SELF_SEM = {"vector": "vE", "scalar": "sE"}

    def run(self, eng, fn, sem=None):
        own = self.SELF_SEM.get(eng)
        if own is not None and self.ct[own] > 0:
            self.wait(eng, own, self.ct[own])
        if sem is None:
            self.items.append((eng, fn))
            return None
        self.ct[sem] += 1
        s = self.sems[sem]
        self.items.append((eng, lambda e, s=s, f=fn: f(e).then_inc(s, 1)))
        return self.ct[sem]

    def join(self, engines=None):
        snap = dict(self.ct)
        for eng in (engines or self.ENGINES):
            for sem, v in snap.items():
                if v > 0:
                    self.wait(eng, sem, v)

    def replay(self, block):
        per = {e: [fn for (ee, fn) in self.items if ee == e] for e in self.ENGINES}

        @block.sync
        def _(e):
            for fn in per["sync"]:
                fn(e)

        @block.scalar
        def _(e):
            for fn in per["scalar"]:
                fn(e)

        @block.vector
        def _(e):
            for fn in per["vector"]:
                fn(e)

        @block.gpsimd
        def _(e):
            for fn in per["gpsimd"]:
                fn(e)

        @block.tensor
        def _(e):
            for fn in per["tensor"]:
                fn(e)


def build_nc(tap=None):
    nc = bass.Bass("TRN2", target_bir_lowering=False, detect_race_conditions=False)

    def din(name, shape):
        return nc.dram_tensor(name, shape, F32, kind="ExternalInput").ap()

    x_h = din("x_fm", [D, T])
    w_in_h = din("w_in", [16, 128, 8 * 128])
    w_gate_h = din("w_gate", [16, 128, 8 * 128])
    w_b_h = din("w_b", [16, 128, 16 * 128])
    w_c_h = din("w_c", [16, 128, 16 * 128])
    w_d_h = din("w_d", [16, 128, 16 * 128])
    w_out_h = din("w_out", [8, 128, 16 * 128])
    w_fg_h = din("w_fg", [8, 128, 8 * 512])
    w_fu_h = din("w_fu", [8, 128, 8 * 512])
    w_fd_h = din("w_fd", [2, 128, 32 * 512])
    bias_h = din("biasp", [128, 112])
    mask_h = din("maskc", [128, 1])
    ident_h = din("ident_in", [128, 128])
    out_h = nc.dram_tensor("out_fm", [D, T], F32, kind="ExternalOutput").ap()

    def scr(name, shape, dt):
        return nc.dram_tensor(name, shape, dt, kind="Internal").ap()

    s_in = scr("s_in", [16, 128, 8 * 128], BF16)
    s_gate = scr("s_gate", [16, 128, 8 * 128], BF16)
    s_b = scr("s_b", [16, 128, 16 * 128], BF16)
    s_c = scr("s_c", [16, 128, 16 * 128], BF16)
    s_d = scr("s_d", [16, 128, 16 * 128], BF16)
    s_out = scr("s_out", [8, 128, 16 * 128], BF16)
    s_fg = scr("s_fg", [8, 128, 8 * 512], BF16)
    s_fu = scr("s_fu", [8, 128, 8 * 512], BF16)
    s_fd = scr("s_fd", [2, 128, 32 * 512], BF16)
    u_sp = scr("u_sp", [NS, 128, IC * SA], BF16)
    b_sp = scr("b_sp", [NS, IC, 128, SA], F32)
    x2_sp = scr("x2_sp", [NS, 128, DC * SA], F32)
    xq1_sp = scr("xq1_sp", [NS, 128, DC * SA], BF16)
    inv1_sp = scr("inv1_sp", [NS, SA], F32)
    cc_in = scr("cc_in", [128, 16], F32)
    cc_out = scr("cc_out", [256, 16], F32)

    AR_N = 44032
    arena = nc.alloc_sbuf_tensor("arena", [128, AR_N], F32).ap()

    def reg(off, n, dt=F32):
        assert off + n <= AR_N, (off, n)
        ap = arena[:, off:off + n]
        if dt == BF16:
            return ap.bitcast(BF16)
        return ap

    X_T = reg(0, 8192)
    XQ1_T = reg(0, 4096, BF16)
    U_T = reg(8192, 8192, BF16)
    XB_T = reg(8192, 4096, BF16)
    XSQ_T = reg(12288, 4096, BF16)
    XN_T = reg(16384, 4096, BF16)
    Y_T = reg(20480, 8192, BF16)
    YQF_T = reg(20480, 8192)
    X2_T = reg(28672, 8192)
    WS = [reg(36864 + i * 1024, 1024, BF16) for i in range(4)]
    BW = [reg(40960 + i * 1024, 1024) for i in range(2)]
    SCO_A = reg(20480, 1024)
    TMPA_A = [reg(21504, 1024), reg(22528, 1024)]
    SCO_B = reg(16384, 1024)
    TMPA_B = [reg(17408, 1024), reg(18432, 1024)]
    TMP3_B = reg(19456, 1024)
    MAXC = reg(36864, 64)
    PT_F = [reg(0, 4096), reg(4096, 4096)]
    PT_B = [reg(8192, 2048, BF16), reg(10240, 2048, BF16)]
    C_WF = [reg(i * 2048, 2048, BF16) for i in range(4)]
    C_XQ1 = reg(8192, 2048, BF16)
    C_GU = reg(10240, 16384)
    C_WD = [reg(10240, 4096, BF16),
            reg(14336, 4096, BF16)]
    C_XQ2 = reg(26624, 8192, BF16)
    C_UPST = reg(26624, 2048)
    C_X2 = reg(34816, 4096)
    C_BC = reg(38912, 1536)
    C_TMP = reg(40448, 512)
    C_GS = reg(40960, 512)
    C_UP = reg(41472, 512)
    MAXC_C = reg(41984, 64)

    rows = nc.alloc_sbuf_tensor("rows", [1, 4 * 1024], F32).ap()
    R = [rows[0:1, i * 1024:(i + 1) * 1024] for i in range(4)]
    rows_r = nc.alloc_sbuf_tensor("rows_r", [1, 3 * 1024], F32R).ap()
    RR = [rows_r[0:1, i * 1024:(i + 1) * 1024] for i in range(3)]

    bias_t = nc.alloc_sbuf_tensor("bias_t", [128, 112], F32).ap()
    IN_B, GATE_B = bias_t[:, 0:16], bias_t[:, 16:32]
    B_B, C_B, D_B = bias_t[:, 32:48], bias_t[:, 48:64], bias_t[:, 64:80]
    OUT_B, NW, ALOG = bias_t[:, 80:88], bias_t[:, 88:96], bias_t[:, 96:112]
    nnw_t = nc.alloc_sbuf_tensor("nnw", [128, 8], F32).ap()
    a_t = nc.alloc_sbuf_tensor("a_t", [128, 16], F32).ap()
    a_tmp = nc.alloc_sbuf_tensor("a_tmp", [128, 64], F32).ap()
    a_msk = nc.alloc_sbuf_tensor("a_msk", [128, 16], U32).ap()
    state_t = nc.alloc_sbuf_tensor("state", [128, 16], F32).ap()
    hprev_t = nc.alloc_sbuf_tensor("hprev", [128, 16], F32).ap()
    mask_t = nc.alloc_sbuf_tensor("maskt", [128, 1], F32).ap()
    ones_bf = nc.alloc_sbuf_tensor("ones_bf", [128, 2], BF16).ap()
    ones_r = nc.alloc_sbuf_tensor("ones_r", [1, 128], F32R).ap()
    ones_f = nc.alloc_sbuf_tensor("ones_f", [1, 128], F32).ap()
    onec_f = nc.alloc_sbuf_tensor("onec_f", [128, 2], F32).ap()
    ident = nc.alloc_sbuf_tensor("ident", [128, 128], F32).ap()
    mw_t = nc.alloc_sbuf_tensor("mw_t", [128, 8], F32).ap()
    wsc_t = nc.alloc_sbuf_tensor("wsc_t", [1, 8], F32).ap()
    acc_t = nc.alloc_sbuf_tensor("acc_t", [128, 8], F32).ap()
    inv1_row = nc.alloc_sbuf_tensor("inv1_row", [1, 512], F32).ap()

    psA = nc.alloc_psum_tensor("psA", [128, 2048], F32).ap()
    psB = nc.alloc_psum_tensor("psB", [128, 2048], F32).ap()

    es = ExitStack()
    semnames = ["sq", "aq", "gq", "tE", "vE", "sE", "cc"]
    sems = {k: es.enter_context(nc.semaphore(k)) for k in semnames}
    p = Prog(nc, sems)
    tap_state = {"done": False}

    def tap_out(name, ap, shape, dt=F32):
        if tap != name or tap_state["done"]:
            return
        tap_state["done"] = True
        th = nc.dram_tensor("dbg", list(shape), dt, kind="ExternalOutput").ap()
        p.join()
        p.dma("sync", "sq", th[:], ap)
        p.join()

    def emit_rsqrt(src, dst, scratch):
        p.run("scalar", lambda e: e.activation(scratch, src, AF.Ln), "sE")
        p.run("scalar", lambda e: e.activation(dst, scratch, AF.Exp, scale=-0.5), "sE")
        p.join(["vector"])
        for _ in range(2):
            p.run("vector", lambda e: e.tensor_tensor(scratch, dst, dst, ALU.mult), "vE")
            p.run("vector", lambda e: e.tensor_tensor(scratch, scratch, src, ALU.mult), "vE")
            p.run("vector", lambda e: e.tensor_scalar(
                scratch, scratch, -0.5, 1.5, ALU.mult, ALU.add), "vE")
            p.run("vector", lambda e: e.tensor_tensor(dst, dst, scratch, ALU.mult), "vE")

    def emit_stats(xb, xsq, var_mult, var_eps, n):
        for src, base in ((xb, 0), (xsq, 1024)):
            for k in range(DC):
                for h in range(n // 512):
                    p.run("tensor", lambda e, s=src, k=k, h=h, b=base: e.matmul(
                        psA[0:1, b + h * 512:b + (h + 1) * 512],
                        ones_bf[:, 0:1],
                        s[:, k * n + h * 512:k * n + (h + 1) * 512],
                        start=(k == 0), stop=(k == DC - 1)), "tE")
        p.wait("vector", "tE")
        p.run("vector", lambda e: e.tensor_scalar(
            R[0][0:1, 0:n], psA[0:1, 0:n], RCP1024, None, ALU.mult), "vE")
        p.run("vector", lambda e: e.tensor_scalar(
            R[1][0:1, 0:n], psA[0:1, 1024:1024 + n], RCP1024, None, ALU.mult), "vE")
        p.run("vector", lambda e: e.tensor_tensor(
            R[3][0:1, 0:n], R[0][0:1, 0:n], R[0][0:1, 0:n], ALU.mult), "vE")
        p.run("vector", lambda e: e.tensor_tensor(
            R[1][0:1, 0:n], R[1][0:1, 0:n], R[3][0:1, 0:n], ALU.subtract), "vE")
        p.run("vector", lambda e: e.tensor_scalar(
            R[1][0:1, 0:n], R[1][0:1, 0:n], var_mult, var_eps, ALU.mult, ALU.add), "vE")
        p.join(["scalar"])
        emit_rsqrt(R[1][0:1, 0:n], R[2][0:1, 0:n], R[3][0:1, 0:n])

    def emit_bcast(ps, pb_off, row_r, n):
        for h in range(n // 512):
            p.run("tensor", lambda e, h=h: e.matmul(
                ps[:, pb_off + h * 512:pb_off + (h + 1) * 512],
                ones_r[0:1, :], row_r[0:1, h * 512:(h + 1) * 512],
                start=True, stop=True), "tE")

    p.dma("sync", "sq", bias_t[:], bias_h[:])
    p.dma("sync", "sq", mask_t[:], mask_h[:])
    p.dma("sync", "sq", ident[:], ident_h[:])
    for (dst, src) in [(s_in, w_in_h), (s_gate, w_gate_h), (s_b, w_b_h),
                       (s_c, w_c_h), (s_d, w_d_h), (s_out, w_out_h)]:
        p.dma("gpsimd", "gq", dst[:], src[:])
    for mset in [ones_bf, ones_f, onec_f]:
        p.run("vector", lambda e, m=mset: e.memset(m[:], 1.0), "vE")
    p.run("vector", lambda e: e.tensor_copy(ones_r[:], ones_f[:]), "vE")
    p.run("vector", lambda e: e.memset(state_t[:], 0.0), "vE")
    p.run("vector", lambda e: e.memset(acc_t[:], 0.0), "vE")
    p.wait("vector", "sq", 16)
    p.run("vector", lambda e: e.tensor_scalar(nnw_t[:], NW, -1.0, None, ALU.mult), "vE")

    AT = [a_tmp[:, i * 16:(i + 1) * 16] for i in range(4)]
    p.join(["scalar"])
    p.run("scalar", lambda e: e.activation(AT[0], ALOG, AF.Abs), "sE")
    p.run("scalar", lambda e: e.activation(AT[0], AT[0], AF.Exp, scale=-1.0), "sE")
    p.run("scalar", lambda e: e.activation(AT[1], AT[0], AF.Ln, bias=1.0), "sE")
    p.join(["vector"])
    p.run("vector", lambda e: e.tensor_scalar(AT[2], AT[0], -0.25, 1.0 / 3.0, ALU.mult, ALU.add), "vE")
    p.run("vector", lambda e: e.tensor_tensor(AT[2], AT[2], AT[0], ALU.mult), "vE")
    p.run("vector", lambda e: e.tensor_scalar(AT[2], AT[2], -0.5, None, ALU.add), "vE")
    p.run("vector", lambda e: e.tensor_tensor(AT[2], AT[2], AT[0], ALU.mult), "vE")
    p.run("vector", lambda e: e.tensor_scalar(AT[2], AT[2], 1.0, None, ALU.add), "vE")
    p.run("vector", lambda e: e.tensor_tensor(AT[2], AT[2], AT[0], ALU.mult), "vE")
    p.run("vector", lambda e: e.tensor_scalar(a_msk[:], AT[0], 0.01, None, ALU.is_lt), "vE")
    p.run("vector", lambda e: e.select(AT[3], a_msk[:], AT[2], AT[1]), "vE")
    p.run("vector", lambda e: e.tensor_scalar(AT[0], ALOG, 0.0, None, ALU.max), "vE")
    p.run("vector", lambda e: e.tensor_tensor(AT[3], AT[3], AT[0], ALU.add), "vE")
    p.join(["scalar"])
    p.run("scalar", lambda e: e.activation(a_t[:], AT[3], AF.Exp, scale=-1.0), "sE")
    tap_out("a", a_t[:], [128, 16])

    ffn_tiles = []
    for (src, dst, mi) in [(w_fg_h, s_fg, 0), (w_fu_h, s_fu, 1), (w_fd_h, s_fd, 2)]:
        if mi < 2:
            for gi in range(8):
                ffn_tiles.append((src[gi], dst[gi], mi))
        else:
            for gi in range(2):
                for q in range(4):
                    ffn_tiles.append((src[gi][:, q * 4096:(q + 1) * 4096],
                                      dst[gi][:, q * 4096:(q + 1) * 4096], mi))
    prev_v = 0
    for i, (sap, dap, mi) in enumerate(ffn_tiles):
        par = i % 2
        if i >= 2:
            p.wait("sync", "vE", prev_v)
        lt = p.dma("sync", "sq", PT_F[par][:], sap)
        p.wait("vector", "sq", lt)
        p.run("vector", lambda e, t=PT_F[par]: e.reduce_sum(
            acc_t[:, 6:7], t[:], axis=AXX, apply_absolute_value=True), "vE")
        prev_v = p.run("vector", lambda e, m=mi: e.tensor_tensor(
            acc_t[:, m:m + 1], acc_t[:, m:m + 1], acc_t[:, 6:7], ALU.add), "vE")
    p.join(["tensor"])
    for mi in range(3):
        p.run("tensor", lambda e, m=mi: e.matmul(
            psA[0:1, m * 512:m * 512 + 1], onec_f[:, 0:1], acc_t[:, m:m + 1],
            start=True, stop=True), "tE")
    p.wait("vector", "tE")
    for mi in range(3):
        p.run("vector", lambda e, m=mi: e.tensor_scalar(
            wsc_t[0:1, m:m + 1], psA[0:1, m * 512:m * 512 + 1],
            float(np.float32(2.0 ** -22)), 1e-5, ALU.mult, ALU.max), "vE")
    p.join(["tensor"])
    for mi in range(3):
        p.run("tensor", lambda e, m=mi: e.matmul(
            psB[:, m * 512:m * 512 + 1], ones_f[0:1, :], wsc_t[0:1, m:m + 1],
            start=True, stop=True), "tE")
    p.wait("vector", "tE")
    for mi in range(3):
        p.run("vector", lambda e, m=mi: e.tensor_copy(
            mw_t[:, m:m + 1], psB[:, m * 512:m * 512 + 1]), "vE")
    for mi in range(3):
        p.run("vector", lambda e, m=mi: e.reciprocal(
            mw_t[:, 3 + m:4 + m], mw_t[:, m:m + 1]), "vE")
    p.join()

    prev_v = 0
    prev_g = 0
    for i, (sap, dap, mi) in enumerate(ffn_tiles):
        par = i % 2
        if i >= 2:
            p.wait("sync", "vE", prev_v)
        lt = p.dma("sync", "sq", PT_F[par][:], sap)
        p.wait("vector", "sq", lt)
        if i >= 2:
            p.wait("vector", "gq", prev_g)
        ws = mw_t[:, 3 + mi:4 + mi]
        p.run("vector", lambda e, t=PT_F[par], w=ws: e.tensor_scalar(
            t[:], t[:], w, CLIP15, ALU.mult, ALU.min), "vE")
        p.run("vector", lambda e, t=PT_F[par]: e.tensor_scalar(
            t[:], t[:], -CLIP15, CBIG, ALU.max, ALU.add), "vE")
        prev_v = p.run("vector", lambda e, t=PT_F[par], o=PT_B[par]: e.tensor_scalar(
            o[:], t[:], CBIG, None, ALU.subtract), "vE")
        p.wait("gpsimd", "vE", prev_v)
        prev_g = p.dma("gpsimd", "gq", dap, PT_B[par][:])
    p.join()

    x_r = x_h.rearrange("(c p) t -> p c t", p=128)
    X3 = X_T.rearrange("p (c t) -> p c t", c=DC)
    XB3 = XB_T.rearrange("p (c t) -> p c t", c=DC)

    def emit_ln1_head(s, need_xn, TMPA):
        xsl = x_r[:, :, s * SA:(s + 1) * SA]
        p.dma("sync", "sq", X3[:, :, :], xsl)
        p.dma("gpsimd", "gq", XB3[:, :, :], xsl)
        p.wait("scalar", "sq")
        p.wait("scalar", "gq")
        p.run("scalar", lambda e: e.activation(XSQ_T[:], X_T[:], AF.Square), "sE")
        p.wait("tensor", "gq")
        p.wait("tensor", "sE")
        if need_xn:
            emit_stats(XB_T, XSQ_T, 1.000001, 1e-11, SA)
            p.run("vector", lambda e: e.tensor_copy(RR[0][0:1, 0:SA], R[2][0:1, 0:SA]), "vE")
            p.run("vector", lambda e: e.tensor_tensor(
                R[3][0:1, 0:SA], R[0][0:1, 0:SA], R[2][0:1, 0:SA], ALU.mult), "vE")
            p.run("vector", lambda e: e.tensor_copy(RR[1][0:1, 0:SA], R[3][0:1, 0:SA]), "vE")
            p.join(["tensor"])
            emit_bcast(psB, 0, RR[0], SA)
            emit_bcast(psB, 1024, RR[1], SA)
            p.wait("vector", "tE")
            for c in range(DC):
                p.run("vector", lambda e, c=c: e.scalar_tensor_tensor(
                    TMPA[0][:], X_T[:, c * SA:(c + 1) * SA], NW[:, c:c + 1],
                    psB[:, 0:SA], ALU.mult, ALU.mult), "vE")
                p.run("vector", lambda e, c=c: e.scalar_tensor_tensor(
                    XN_T[:, c * SA:(c + 1) * SA], psB[:, 1024:1024 + SA],
                    nnw_t[:, c:c + 1], TMPA[0][:], ALU.mult, ALU.add), "vE")
        else:
            emit_stats(XB_T, XSQ_T, 1.0, 1e-5, SA)
            p.run("vector", lambda e: e.tensor_tensor(
                R[3][0:1, 0:SA], R[0][0:1, 0:SA], R[2][0:1, 0:SA], ALU.mult), "vE")
            p.run("vector", lambda e: e.tensor_copy(RR[1][0:1, 0:SA], R[3][0:1, 0:SA]), "vE")
            p.run("vector", lambda e: e.tensor_scalar(
                R[3][0:1, 0:SA], R[2][0:1, 0:SA], 1.0, None, ALU.add), "vE")
            p.run("vector", lambda e: e.tensor_copy(RR[2][0:1, 0:SA], R[3][0:1, 0:SA]), "vE")

    def emit_mm_loop(nm, kc, wsrc_fn, rhs, n, evac_fn):
        dmatok = [0] * nm
        mmtok = [0] * nm
        etok = [None] * nm
        for m in range(min(2, nm)):
            dmatok[m] = p.dma("sync", "sq", WS[m % 2][:, 0:kc * 128], wsrc_fn(m))
        for m in range(nm):
            par = m % 2
            ps = psA if par == 0 else psB
            p.wait("tensor", "sq", dmatok[m])
            if m >= 2 and etok[m - 2] is not None:
                p.wait("tensor", etok[m - 2][0], etok[m - 2][1])
            tok = None
            for k in range(kc):
                for h in range(n // 512):
                    tok = p.run("tensor", lambda e, ps=ps, k=k, h=h, par=par: e.matmul(
                        ps[:, h * 512:(h + 1) * 512],
                        WS[par][:, k * 128:(k + 1) * 128],
                        rhs[:, k * n + h * 512:k * n + (h + 1) * 512],
                        start=(k == 0), stop=(k == kc - 1)), "tE")
            mmtok[m] = tok
            if m + 2 < nm:
                p.wait("sync", "tE", mmtok[m])
                dmatok[m + 2] = p.dma("sync", "sq", WS[par][:, 0:kc * 128], wsrc_fn(m + 2))
            etok[m] = evac_fn(m, ps)
        return mmtok

    def emit_mm_loop2(nm, kc, wsrc1, wsrc2, rhs, n, evac_fn, pre_fn=None):
        dmatok = [[0] * nm, [0] * nm]
        mmtok = [0] * nm
        etok = [None] * nm
        for m in range(min(2, nm)):
            dmatok[0][m] = p.dma("sync", "sq", WS[m % 2][:, 0:kc * 128], wsrc1(m))
            dmatok[1][m] = p.dma("sync", "sq", WS[2 + m % 2][:, 0:kc * 128], wsrc2(m))
        for m in range(nm):
            par = m % 2
            ps = psA if par == 0 else psB
            if pre_fn is not None:
                pre_fn(m)
            p.wait("tensor", "sq", max(dmatok[0][m], dmatok[1][m]))
            if m >= 2 and etok[m - 2] is not None:
                p.wait("tensor", etok[m - 2][0], etok[m - 2][1])
            tok = None
            for wi, wb in ((0, WS[par]), (1, WS[2 + par])):
                for k in range(kc):
                    for h in range(n // 512):
                        tok = p.run("tensor", lambda e, ps=ps, k=k, h=h, wb=wb, wi=wi: e.matmul(
                            ps[:, wi * 1024 + h * 512:wi * 1024 + (h + 1) * 512],
                            wb[:, k * 128:(k + 1) * 128],
                            rhs[:, k * n + h * 512:k * n + (h + 1) * 512],
                            start=(k == 0), stop=(k == kc - 1)), "tE")
            mmtok[m] = tok
            if m + 2 < nm:
                p.wait("sync", "tE", mmtok[m])
                dmatok[0][m + 2] = p.dma("sync", "sq", WS[par][:, 0:kc * 128], wsrc1(m + 2))
                dmatok[1][m + 2] = p.dma("sync", "sq", WS[2 + par][:, 0:kc * 128], wsrc2(m + 2))
            etok[m] = evac_fn(m, ps)
        return mmtok

    for s in range(NS):
        p.join()
        emit_ln1_head(s, True, TMPA_A)
        tap_out("xn" if s == 0 else "_xn%d" % s, XN_T[:], [128, 8192], BF16)
        p.join()

        def evac_ug(m, ps):
            par = m % 2
            p.wait("scalar", "tE")
            p.run("scalar", lambda e, ps=ps, m=m, par=par: e.activation(
                TMPA_A[par][:], ps[:, 0:SA], AF.Identity, bias=IN_B[:, m:m + 1]), "sE")
            tok = p.run("scalar", lambda e, ps=ps, m=m, par=par: e.activation(
                BW[par][:], ps[:, 1024:1024 + SA], AF.Sigmoid, bias=GATE_B[:, m:m + 1]), "sE")
            p.wait("vector", "sE")
            p.run("vector", lambda e, m=m, par=par: e.tensor_tensor(
                U_T[:, m * SA:(m + 1) * SA], TMPA_A[par][:], BW[par][:], ALU.mult), "vE")
            return ("sE", tok)

        emit_mm_loop2(IC, DC, lambda m: s_in[m], lambda m: s_gate[m],
                      XN_T, SA, evac_ug)
        p.join()
        tap_out("u" if s == 0 else "_u%d" % s, U_T[:], [128, 16384], BF16)
        p.dma("gpsimd", "gq", u_sp[s], U_T[:])

        def evac_b(m, ps, s=s):
            par = m % 2
            if m >= 2:
                p.wait("scalar", "vE")
                p.wait("scalar", "gq")
            p.wait("scalar", "tE")
            tok = p.run("scalar", lambda e, ps=ps, m=m, par=par: e.activation(
                BW[par][:], ps[:, 0:SA], AF.Identity, bias=B_B[:, m:m + 1]), "sE")
            p.wait("gpsimd", "sE")
            p.dma("gpsimd", "gq", b_sp[s, m], BW[par][:])
            p.wait("vector", "sE")
            p.run("vector", lambda e, m=m, par=par: e.tensor_tensor_scan(
                SCO_A[:], a_t[:, m:m + 1].broadcast_to((128, SA)), BW[par][:],
                state_t[:, m:m + 1], ALU.mult, ALU.add), "vE")
            p.run("vector", lambda e, m=m: e.tensor_copy(
                state_t[:, m:m + 1], SCO_A[:, SA - 1:SA]), "vE")
            return ("sE", tok)

        emit_mm_loop(IC, IC, lambda m: s_b[m], U_T, SA, evac_b)
        p.join()
    tap_out("hend", state_t[:], [128, 16])

    p.join()
    p.dma("gpsimd", "gq", cc_in[:], state_t[:])
    p.wait("gpsimd", "gq")
    p.run("gpsimd", lambda e: e.collective_compute(
        "AllGather", ALU.bypass,
        replica_groups=CCGROUPS,
        ins=[cc_in[:]], outs=[cc_out[:]]), "cc")
    p.wait("gpsimd", "cc")
    p.dma("gpsimd", "gq", hprev_t[:], cc_out[0:128, :])
    p.wait("vector", "gq")
    p.run("vector", lambda e: e.tensor_scalar(
        state_t[:], hprev_t[:], mask_t[:, 0:1], None, ALU.mult), "vE")
    p.join()
    tap_out("hseed", state_t[:], [128, 16])

    for s in range(NS):
        p.join()
        emit_ln1_head(s, False, TMPA_B)
        p.join()
        p.dma("sync", "sq", U_T[:], u_sp[s])

        btok = {}

        def pre_cd(m, s=s, btok=btok):
            if m == 0:
                btok[0] = p.dma("sync", "aq", BW[0][:], b_sp[s, 0])
            if m + 1 < IC:
                p.wait("sync", "vE")
                btok[m + 1] = p.dma("sync", "aq", BW[(m + 1) % 2][:], b_sp[s, m + 1])
            p.wait("vector", "aq", btok[m])
            p.run("vector", lambda e, m=m: e.tensor_tensor_scan(
                SCO_B[:], a_t[:, m:m + 1].broadcast_to((128, SA)), BW[m % 2][:],
                state_t[:, m:m + 1], ALU.mult, ALU.add), "vE")
            p.run("vector", lambda e, m=m: e.tensor_copy(
                state_t[:, m:m + 1], SCO_B[:, SA - 1:SA]), "vE")

        def evac_cd(m, ps):
            par = m % 2
            p.wait("scalar", "tE")
            p.run("scalar", lambda e, ps=ps, m=m, par=par: e.activation(
                TMPA_B[par][:], ps[:, 0:SA], AF.Identity, bias=C_B[:, m:m + 1]), "sE")
            tok = p.run("scalar", lambda e, ps=ps, m=m: e.activation(
                TMP3_B[:], ps[:, 1024:1024 + SA], AF.Identity, bias=D_B[:, m:m + 1]), "sE")
            p.wait("vector", "sE")
            p.run("vector", lambda e, par=par: e.tensor_tensor(
                TMPA_B[par][:], TMPA_B[par][:], SCO_B[:], ALU.mult), "vE")
            p.run("vector", lambda e, m=m, par=par: e.tensor_tensor(
                Y_T[:, m * SA:(m + 1) * SA], TMPA_B[par][:], TMP3_B[:], ALU.add), "vE")
            return ("sE", tok)

        p.wait("tensor", "sq")
        emit_mm_loop2(IC, IC, lambda m: s_c[m], lambda m: s_d[m],
                      U_T, SA, evac_cd, pre_fn=pre_cd)
        p.join()
        tap_out("y" if s == 0 else "_y%d" % s, Y_T[:], [128, 16384], BF16)

        def evac_out(m, ps):
            p.wait("scalar", "tE")
            tok = p.run("scalar", lambda e, ps=ps, m=m: e.activation(
                X2_T[:, m * SA:(m + 1) * SA], ps[:, 0:SA], AF.Identity,
                bias=OUT_B[:, m:m + 1]), "sE")
            return ("sE", tok)

        emit_mm_loop(DC, IC, lambda m: s_out[m], Y_T, SA, evac_out)
        p.join()

        emit_bcast(psB, 0, RR[2], SA)
        emit_bcast(psB, 1024, RR[1], SA)
        p.wait("vector", "tE")
        for c in range(DC):
            p.run("vector", lambda e, c=c: e.tensor_tensor(
                TMPA_B[0][:], X_T[:, c * SA:(c + 1) * SA], psB[:, 0:SA], ALU.mult), "vE")
            p.run("vector", lambda e, c=c: e.scalar_tensor_tensor(
                TMPA_B[0][:], psB[:, 1024:1024 + SA], -1.0, TMPA_B[0][:],
                ALU.mult, ALU.add), "vE")
            p.run("vector", lambda e, c=c: e.tensor_tensor(
                X2_T[:, c * SA:(c + 1) * SA], X2_T[:, c * SA:(c + 1) * SA],
                TMPA_B[0][:], ALU.add), "vE")
        p.join()
        tap_out("x2" if s == 0 else "_x2%d" % s, X2_T[:], [128, 8192])
        p.dma("gpsimd", "gq", x2_sp[s], X2_T[:])

        p.run("vector", lambda e: e.tensor_copy(XB_T[:], X2_T[:]), "vE")
        p.run("scalar", lambda e: e.activation(XSQ_T[:], X2_T[:], AF.Square), "sE")
        p.join(["tensor"])
        emit_stats(XB_T, XSQ_T, 1.0, 1e-5, SA)
        p.run("vector", lambda e: e.tensor_copy(RR[0][0:1, 0:SA], R[0][0:1, 0:SA]), "vE")
        p.join(["tensor"])
        emit_bcast(psB, 0, RR[0], SA)
        p.wait("vector", "tE")
        for c in range(DC):
            p.run("vector", lambda e, c=c: e.scalar_tensor_tensor(
                X_T[:, c * SA:(c + 1) * SA], psB[:, 0:SA], -1.0,
                X2_T[:, c * SA:(c + 1) * SA], ALU.mult, ALU.add), "vE")
        p.join(["tensor"])
        slots = [(psA, 0), (psA, 1024), (psB, 0), (psB, 1024)]
        vtoks = {}
        for blk in range(SA // 128):
            for c in range(DC):
                i = blk * DC + c
                ps_s, sl = slots[i % 4]
                if i >= 4:
                    p.wait("tensor", "vE", vtoks[i - 4])
                p.run("tensor", lambda e, ps_s=ps_s, sl=sl, c=c, blk=blk: e.transpose(
                    ps_s[:, sl:sl + 128],
                    X_T[:, c * SA + blk * 128:c * SA + (blk + 1) * 128],
                    ident[:]), "tE")
                p.wait("vector", "tE")
                vtoks[i] = p.run("vector", lambda e, ps_s=ps_s, sl=sl, c=c: e.reduce_max(
                    MAXC[:, 8 + c:9 + c], ps_s[:, sl:sl + 128], axis=AXX,
                    apply_absolute_value=True), "vE")
            p.run("vector", lambda e, blk=blk: e.reduce_max(
                MAXC[:, blk:blk + 1], MAXC[:, 8:16], axis=AXX), "vE")
        p.join(["tensor"])
        for blk in range(SA // 128):
            ps_s, sl = slots[blk % 4]
            if blk >= 4:
                p.wait("tensor", "vE")
            p.run("tensor", lambda e, ps_s=ps_s, sl=sl, blk=blk: e.transpose(
                ps_s[0:1, sl:sl + 128], MAXC[:, blk:blk + 1], ident[:]), "tE")
            p.wait("vector", "tE")
            p.run("vector", lambda e, ps_s=ps_s, sl=sl, blk=blk: e.tensor_copy(
                R[3][0:1, blk * 128:(blk + 1) * 128], ps_s[0:1, sl:sl + 128]), "vE")
        p.run("vector", lambda e: e.tensor_tensor(
            R[3][0:1, 0:SA], R[3][0:1, 0:SA], R[2][0:1, 0:SA], ALU.mult), "vE")
        p.run("vector", lambda e: e.tensor_scalar(
            R[3][0:1, 0:SA], R[3][0:1, 0:SA], 1e-5, None, ALU.max), "vE")
        p.run("vector", lambda e: e.tensor_scalar(
            R[1][0:1, 0:SA], R[3][0:1, 0:SA], float(np.float32(1.0 / 127.0)),
            None, ALU.mult), "vE")
        p.wait("gpsimd", "vE")
        sp_tok = p.dma("gpsimd", "gq", inv1_sp[s:s + 1, :], R[1][0:1, 0:SA])
        p.wait("vector", "gq", sp_tok)
        p.run("vector", lambda e: e.reciprocal(R[1][0:1, 0:SA], R[3][0:1, 0:SA]), "vE")
        p.run("vector", lambda e: e.tensor_scalar(
            R[1][0:1, 0:SA], R[1][0:1, 0:SA], 127.0, None, ALU.mult), "vE")
        p.run("vector", lambda e: e.tensor_tensor(
            R[1][0:1, 0:SA], R[1][0:1, 0:SA], R[2][0:1, 0:SA], ALU.mult), "vE")
        p.run("vector", lambda e: e.tensor_copy(RR[1][0:1, 0:SA], R[1][0:1, 0:SA]), "vE")
        p.join(["tensor"])
        emit_bcast(psB, 0, RR[1], SA)
        p.wait("vector", "tE")
        for c in range(DC):
            p.run("vector", lambda e, c=c: e.tensor_tensor(
                YQF_T[:, c * SA:(c + 1) * SA], X_T[:, c * SA:(c + 1) * SA],
                psB[:, 0:SA], ALU.mult), "vE")
            p.run("vector", lambda e, c=c: e.tensor_scalar(
                XQ1_T[:, c * SA:(c + 1) * SA], YQF_T[:, c * SA:(c + 1) * SA],
                CBIG, CBIG, ALU.add, ALU.subtract), "vE")
        p.join(["gpsimd"])
        p.dma("gpsimd", "gq", xq1_sp[s], XQ1_T[:])
        tap_out("xq1" if s == 0 else "_xq1%d" % s, XQ1_T[:], [128, 8192], BF16)

    for sc in range(NSC):
        s, t0 = sc // 2, (sc % 2) * SC
        p.join()
        xq1s = xq1_sp[s].rearrange("p (c t) -> p c t", c=DC)[:, :, t0:t0 + SC]
        x2s = x2_sp[s].rearrange("p (c t) -> p c t", c=DC)[:, :, t0:t0 + SC]
        p.dma("sync", "sq", C_XQ1.rearrange("p (c t) -> p c t", c=DC), xq1s)
        p.dma("sync", "sq", C_X2.rearrange("p (c t) -> p c t", c=DC), x2s)
        p.dma("sync", "aq", inv1_row[0:1, 0:SC], inv1_sp[s:s + 1, t0:t0 + SC])
        p.wait("vector", "aq")
        p.run("vector", lambda e: e.tensor_copy(RR[0][0:1, 0:SC], inv1_row[0:1, 0:SC]), "vE")
        p.join(["tensor"])
        emit_bcast(psA, 0, RR[0], SC)
        p.wait("vector", "tE")
        p.run("vector", lambda e: e.tensor_copy(C_BC[:, 0:SC], psA[:, 0:SC]), "vE")
        p.join()

        gdma = [[0] * 8, [0] * 8]
        for gi in range(2):
            gdma[0][gi] = p.dma("sync", "sq", C_WF[gi % 2][:], s_fg[gi])
            gdma[1][gi] = p.dma("sync", "aq", C_WF[2 + gi % 2][:], s_fu[gi])
        evtok = 0
        for gi in range(8):
            par = gi % 2
            if gi >= 1:
                p.wait("tensor", "vE", evtok)
            p.wait("tensor", "sq", gdma[0][gi])
            p.wait("tensor", "aq", gdma[1][gi])
            tok = None
            for wb, ps in ((C_WF[par], psA), (C_WF[2 + par], psB)):
                for k in range(DC):
                    for ms in range(4):
                        tok = p.run("tensor", lambda e, ps=ps, k=k, ms=ms, wb=wb: e.matmul(
                            ps[:, ms * 512:(ms + 1) * 512],
                            wb[:, k * 512 + ms * 128:k * 512 + (ms + 1) * 128],
                            C_XQ1[:, k * SC:(k + 1) * SC],
                            start=(k == 0), stop=(k == DC - 1)), "tE")
            if gi + 2 < 8:
                p.wait("sync", "tE", tok)
                p.wait("sync", "vE", evtok)
                gdma[0][gi + 2] = p.dma("sync", "sq", C_WF[par][:], s_fg[gi + 2])
                gdma[1][gi + 2] = p.dma("sync", "aq", C_WF[2 + par][:], s_fu[gi + 2])
            p.wait("vector", "tE", tok)
            for ms in range(4):
                p.run("vector", lambda e, gi=gi, ms=ms: e.scalar_tensor_tensor(
                    C_GU[:, gi * 2048 + ms * 512:gi * 2048 + (ms + 1) * 512],
                    psA[:, ms * 512:(ms + 1) * 512],
                    mw_t[:, 0:1], C_BC[:, 0:SC], ALU.mult, ALU.mult), "vE")
            for ms in range(4):
                evtok = p.run("vector", lambda e, ms=ms: e.scalar_tensor_tensor(
                    C_UPST[:, ms * 512:(ms + 1) * 512],
                    psB[:, ms * 512:(ms + 1) * 512],
                    mw_t[:, 1:2], C_BC[:, 0:SC], ALU.mult, ALU.mult), "vE")
            for ms in range(4):
                p.wait("scalar", "vE")
                p.run("scalar", lambda e, gi=gi, ms=ms: e.activation(
                    C_GS[:], C_GU[:, gi * 2048 + ms * 512:gi * 2048 + (ms + 1) * 512],
                    AF.Silu), "sE")
                p.wait("vector", "sE")
                p.run("vector", lambda e, gi=gi, ms=ms: e.tensor_tensor(
                    C_GU[:, gi * 2048 + ms * 512:gi * 2048 + (ms + 1) * 512],
                    C_GS[:], C_UPST[:, ms * 512:(ms + 1) * 512], ALU.mult), "vE")
        p.join()
        tap_out("gu" if sc == 0 else "_gu%d" % sc, C_GU[:], [128, 16384])

        slots = [(psA, 0), (psA, 1024), (psB, 0), (psB, 1024)]
        vtoks = {}
        for blk in range(SC // 128):
            for fc in range(FC):
                i = blk * FC + fc
                ps_s, sl = slots[i % 4]
                if i >= 4:
                    p.wait("tensor", "vE", vtoks[i - 4])
                p.run("tensor", lambda e, ps_s=ps_s, sl=sl, fc=fc, blk=blk: e.transpose(
                    ps_s[:, sl:sl + 128],
                    C_GU[:, fc * SC + blk * 128:fc * SC + (blk + 1) * 128],
                    ident[:]), "tE")
                p.wait("vector", "tE")
                vtoks[i] = p.run("vector", lambda e, ps_s=ps_s, sl=sl, fc=fc: e.reduce_max(
                    MAXC_C[:, 16 + fc:17 + fc], ps_s[:, sl:sl + 128],
                    axis=AXX, apply_absolute_value=True), "vE")
            p.run("vector", lambda e, blk=blk: e.reduce_max(
                MAXC_C[:, blk:blk + 1], MAXC_C[:, 16:48], axis=AXX), "vE")
        p.join(["tensor"])
        for blk in range(SC // 128):
            ps_s, sl = slots[blk % 4]
            if blk >= 4:
                p.wait("tensor", "vE")
            p.run("tensor", lambda e, ps_s=ps_s, sl=sl, blk=blk: e.transpose(
                ps_s[0:1, sl:sl + 128], MAXC_C[:, blk:blk + 1], ident[:]), "tE")
            p.wait("vector", "tE")
            p.run("vector", lambda e, ps_s=ps_s, sl=sl, blk=blk: e.tensor_copy(
                R[3][0:1, blk * 128:(blk + 1) * 128], ps_s[0:1, sl:sl + 128]), "vE")
        p.run("vector", lambda e: e.tensor_scalar(
            R[3][0:1, 0:SC], R[3][0:1, 0:SC], 1e-5, None, ALU.max), "vE")
        p.run("vector", lambda e: e.tensor_scalar(
            R[1][0:1, 0:SC], R[3][0:1, 0:SC], float(np.float32(1.0 / 127.0)),
            None, ALU.mult), "vE")
        p.run("vector", lambda e: e.tensor_copy(RR[1][0:1, 0:SC], R[1][0:1, 0:SC]), "vE")
        p.run("vector", lambda e: e.reciprocal(R[1][0:1, 0:SC], R[3][0:1, 0:SC]), "vE")
        p.run("vector", lambda e: e.tensor_scalar(
            R[1][0:1, 0:SC], R[1][0:1, 0:SC], 127.0, None, ALU.mult), "vE")
        p.run("vector", lambda e: e.tensor_copy(RR[2][0:1, 0:SC], R[1][0:1, 0:SC]), "vE")
        p.join(["tensor"])
        emit_bcast(psA, 0, RR[2], SC)
        emit_bcast(psA, 512, RR[1], SC)
        p.wait("vector", "tE")
        p.run("vector", lambda e: e.tensor_copy(C_BC[:, 512:1024], psA[:, 0:512]), "vE")
        p.run("vector", lambda e: e.tensor_copy(C_BC[:, 1024:1536], psA[:, 512:1024]), "vE")
        for fc in range(FC):
            p.run("vector", lambda e, fc=fc: e.tensor_tensor(
                C_UP[:], C_GU[:, fc * SC:(fc + 1) * SC], C_BC[:, 512:1024],
                ALU.mult), "vE")
            p.run("vector", lambda e, fc=fc: e.tensor_scalar(
                C_XQ2[:, fc * SC:(fc + 1) * SC], C_UP[:],
                CBIG, CBIG, ALU.add, ALU.subtract), "vE")
        p.join()

        wdma = {(0, 0): p.dma("sync", "sq", C_WD[0][:], s_fd[0][:, 0:8192]),
                (0, 1): p.dma("sync", "sq", C_WD[1][:], s_fd[0][:, 8192:16384])}
        for dg in range(2):
            ps = psA if dg == 0 else psB
            tok = None
            for kh in range(2):
                p.wait("tensor", "sq", wdma[(dg, kh)])
                for k in range(16):
                    gk = kh * 16 + k
                    for ms in range(4):
                        tok = p.run("tensor", lambda e, ps=ps, k=k, ms=ms, kh=kh, gk=gk: e.matmul(
                            ps[:, ms * 512:(ms + 1) * 512],
                            C_WD[kh][:, k * 512 + ms * 128:k * 512 + (ms + 1) * 128],
                            C_XQ2[:, gk * SC:(gk + 1) * SC],
                            start=(gk == 0), stop=(gk == FC - 1)), "tE")
            if dg == 0:
                p.wait("sync", "tE", tok)
                wdma[(1, 0)] = p.dma("sync", "sq", C_WD[0][:], s_fd[1][:, 0:8192])
                wdma[(1, 1)] = p.dma("sync", "sq", C_WD[1][:], s_fd[1][:, 8192:16384])
            p.wait("vector", "tE", tok)
            for ms in range(4):
                c = dg * 4 + ms
                p.run("vector", lambda e, ps=ps, ms=ms: e.scalar_tensor_tensor(
                    C_TMP[:], ps[:, ms * 512:(ms + 1) * 512], mw_t[:, 2:3],
                    C_BC[:, 1024:1536], ALU.mult, ALU.mult), "vE")
                p.run("vector", lambda e, c=c: e.tensor_tensor(
                    C_X2[:, c * SC:(c + 1) * SC], C_X2[:, c * SC:(c + 1) * SC],
                    C_TMP[:], ALU.add), "vE")
        p.join(["sync"])
        out_sl = out_h.rearrange("(c p) t -> p c t", p=128)[:, :, sc * SC:(sc + 1) * SC]
        p.dma("sync", "sq", out_sl, C_X2.rearrange("p (c t) -> p c t", c=DC))

    p.join()
    p.wait("sync", "sq", p.ct["sq"])

    with nc.Block() as block:
        p.replay(block)
    es.close()
    return nc


def _tile_w(wT, kc, mc, mw=128):
    A = wT.reshape(kc, 128, mc, mw).transpose(2, 1, 0, 3)
    return np.ascontiguousarray(A.reshape(mc, 128, kc * mw)).astype(np.float32)


def _col_pack(v, ncol):
    return np.ascontiguousarray(v.reshape(ncol, 128).T).astype(np.float32)


def kernel(**inputs):
    g = {k: np.asarray(v, dtype=np.float32) for k, v in inputs.items()}
    x = g["L0_attn_x"]

    shared = {
        "w_in": _tile_w(np.ascontiguousarray(g["L0_attn_in_w"].T), 8, 16),
        "w_gate": _tile_w(np.ascontiguousarray(g["L0_attn_gate_w"].T), 8, 16),
        "w_b": _tile_w(np.ascontiguousarray(g["L0_attn_b_w"].T), 16, 16),
        "w_c": _tile_w(np.ascontiguousarray(g["L0_attn_c_w"].T), 16, 16),
        "w_d": _tile_w(np.ascontiguousarray(g["L0_attn_d_w"].T), 16, 16),
        "w_out": _tile_w(np.ascontiguousarray(g["L0_attn_out_w"].T), 16, 8),
        "w_fg": _tile_w(np.ascontiguousarray(g["L0_ffn_w_gate"].T), 8, 8, 512),
        "w_fu": _tile_w(np.ascontiguousarray(g["L0_ffn_w_up"].T), 8, 8, 512),
        "w_fd": _tile_w(np.ascontiguousarray(g["L0_ffn_w_down"].T), 32, 2, 512),
        "ident_in": np.eye(128, dtype=np.float32),
    }
    bias = np.zeros((128, 112), np.float32)
    bias[:, 0:16] = _col_pack(g["L0_attn_in_b"], 16)
    bias[:, 16:32] = _col_pack(g["L0_attn_gate_b"], 16)
    bias[:, 32:48] = _col_pack(g["L0_attn_b_b"], 16)
    bias[:, 48:64] = _col_pack(g["L0_attn_c_b"], 16)
    bias[:, 64:80] = _col_pack(g["L0_attn_d_b"], 16)
    bias[:, 80:88] = _col_pack(g["L0_attn_out_b"], 8)
    bias[:, 88:96] = _col_pack(g["L0_attn_norm_w"], 8)
    bias[:, 96:112] = _col_pack(g["L0_attn_a_log"], 16)
    shared["biasp"] = bias

    in_maps = []
    for c in range(8):
        bc, h = c // 2, c % 2
        m = dict(shared)
        m["x_fm"] = np.ascontiguousarray(x[bc, h * T:(h + 1) * T, :].T)
        m["maskc"] = np.full((128, 1), float(h), np.float32)
        in_maps.append(m)

    nc = build_nc()
    res = run_bass_kernel_spmd(nc, in_maps, core_ids=list(range(8)))
    out = np.empty((4, 8192, 1024), np.float32)
    for c in range(8):
        bc, h = c // 2, c % 2
        out[bc, h * T:(h + 1) * T, :] = np.asarray(res.results[c]["out_fm"]).T
    return out


# version 10
# speedup vs baseline: 1.2411x; 1.0898x over previous
import numpy as np
from contextlib import ExitStack

import concourse.bass as bass
import concourse.mybir as mybir
from concourse.bass_utils import run_bass_kernel_spmd

F32 = mybir.dt.float32
F32R = mybir.dt.float32r
BF16 = mybir.dt.bfloat16
U32 = mybir.dt.uint32
AF = mybir.ActivationFunctionType
ALU = mybir.AluOpType
AXX = mybir.AxisListType.X

T = 4096
SA = 1024
NS = T // SA
SC = 512
NSC = T // SC
D, DI, DF = 1024, 2048, 4096
DC, IC, FC = 8, 16, 32
RCP1024 = float(np.float32(2.0 ** -10))
CBIG = float(np.float32(3 * 2 ** 22))
CLIP15 = 1.4999999
CCGROUPS = [[0, 1], [2, 3], [4, 5], [6, 7]]


class Prog:
    ENGINES = ("sync", "scalar", "vector", "gpsimd", "tensor")

    def __init__(self, nc, sems):
        self.nc = nc
        self.sems = sems
        self.ct = {k: 0 for k in sems}
        self.items = []

    def wait(self, eng, sem, val=None):
        if val is None:
            val = self.ct[sem]
        if val <= 0:
            return
        s = self.sems[sem]
        self.items.append((eng, lambda e, s=s, v=val: e.wait_ge(s, v)))

    def dma(self, eng, sem, out, in_):
        self.ct[sem] += 16
        s = self.sems[sem]
        self.items.append(
            (eng, lambda e, s=s, o=out, i=in_: e.dma_start(out=o, in_=i).then_inc(s, 16)))
        return self.ct[sem]

    SELF_SEM = {"vector": "vE", "scalar": "sE"}

    def run(self, eng, fn, sem=None):
        own = self.SELF_SEM.get(eng)
        if own is not None and self.ct[own] > 0:
            self.wait(eng, own, self.ct[own])
        if sem is None:
            self.items.append((eng, fn))
            return None
        self.ct[sem] += 1
        s = self.sems[sem]
        self.items.append((eng, lambda e, s=s, f=fn: f(e).then_inc(s, 1)))
        return self.ct[sem]

    def join(self, engines=None):
        snap = dict(self.ct)
        for eng in (engines or self.ENGINES):
            for sem, v in snap.items():
                if v > 0:
                    self.wait(eng, sem, v)

    def replay(self, block):
        per = {e: [fn for (ee, fn) in self.items if ee == e] for e in self.ENGINES}

        @block.sync
        def _(e):
            for fn in per["sync"]:
                fn(e)

        @block.scalar
        def _(e):
            for fn in per["scalar"]:
                fn(e)

        @block.vector
        def _(e):
            for fn in per["vector"]:
                fn(e)

        @block.gpsimd
        def _(e):
            for fn in per["gpsimd"]:
                fn(e)

        @block.tensor
        def _(e):
            for fn in per["tensor"]:
                fn(e)


def build_nc(tap=None):
    nc = bass.Bass("TRN2", target_bir_lowering=False, detect_race_conditions=False)

    def din(name, shape):
        return nc.dram_tensor(name, shape, F32, kind="ExternalInput").ap()

    x_h = din("x_fm", [D, T])
    w_in_h = din("w_in", [16, 128, 8 * 128])
    w_gate_h = din("w_gate", [16, 128, 8 * 128])
    w_b_h = din("w_b", [16, 128, 16 * 128])
    w_c_h = din("w_c", [16, 128, 16 * 128])
    w_d_h = din("w_d", [16, 128, 16 * 128])
    w_out_h = din("w_out", [8, 128, 16 * 128])
    w_fg_h = din("w_fg", [8, 128, 8 * 512])
    w_fu_h = din("w_fu", [8, 128, 8 * 512])
    w_fd_h = din("w_fd", [2, 128, 32 * 512])
    bias_h = din("biasp", [128, 112])
    mask_h = din("maskc", [128, 1])
    ident_h = din("ident_in", [128, 128])
    out_h = nc.dram_tensor("out_fm", [D, T], F32, kind="ExternalOutput").ap()

    def scr(name, shape, dt):
        return nc.dram_tensor(name, shape, dt, kind="Internal").ap()

    s_in = scr("s_in", [16, 128, 8 * 128], BF16)
    s_gate = scr("s_gate", [16, 128, 8 * 128], BF16)
    s_b = scr("s_b", [16, 128, 16 * 128], BF16)
    s_c = scr("s_c", [16, 128, 16 * 128], BF16)
    s_d = scr("s_d", [16, 128, 16 * 128], BF16)
    s_out = scr("s_out", [8, 128, 16 * 128], BF16)
    s_fg = scr("s_fg", [8, 128, 8 * 512], BF16)
    s_fu = scr("s_fu", [8, 128, 8 * 512], BF16)
    s_fd = scr("s_fd", [2, 128, 32 * 512], BF16)
    u_sp = scr("u_sp", [NS, 128, IC * SA], BF16)
    b_sp = scr("b_sp", [NS, IC, 128, SA], F32)
    x2_sp = scr("x2_sp", [NS, 128, DC * SA], F32)
    xq1_sp = scr("xq1_sp", [NS, 128, DC * SA], BF16)
    inv1_sp = scr("inv1_sp", [NS, SA], F32)
    cc_in = scr("cc_in", [128, 16], F32)
    cc_out = scr("cc_out", [256, 16], F32)

    AR_N = 44032
    arena = nc.alloc_sbuf_tensor("arena", [128, AR_N], F32).ap()

    def reg(off, n, dt=F32):
        assert off + n <= AR_N, (off, n)
        ap = arena[:, off:off + n]
        if dt == BF16:
            return ap.bitcast(BF16)
        return ap

    X_T = reg(0, 8192)
    XQ1_T = reg(0, 4096, BF16)
    U_T = reg(8192, 8192, BF16)
    XB_T = reg(8192, 4096, BF16)
    XSQ_T = reg(12288, 4096, BF16)
    XN_T = reg(16384, 4096, BF16)
    Y_T = reg(20480, 8192, BF16)
    YQF_T = reg(20480, 8192)
    X2_T = reg(28672, 8192)
    WS = [reg(36864 + i * 1024, 1024, BF16) for i in range(4)]
    BW = [reg(40960 + i * 1024, 1024) for i in range(2)]
    SCO_A = reg(20480, 1024)
    TMPA_A = [reg(21504, 1024), reg(22528, 1024)]
    SCO_B = reg(16384, 1024)
    TMPA_B = [reg(17408, 1024), reg(18432, 1024)]
    TMP3_B = reg(19456, 1024)
    MAXC = reg(36864, 64)
    PT_F = [reg(0, 4096), reg(4096, 4096)]
    PT_B = [reg(8192, 2048, BF16), reg(10240, 2048, BF16)]
    C_WF = [reg(i * 2048, 2048, BF16) for i in range(4)]
    C_XQ1 = reg(8192, 2048, BF16)
    C_GU = reg(10240, 16384)
    C_WD = [reg(10240, 4096, BF16),
            reg(14336, 4096, BF16)]
    C_XQ2 = reg(26624, 8192, BF16)
    C_UPST = reg(26624, 2048)
    C_X2 = reg(34816, 4096)
    C_BC = reg(38912, 1536)
    C_TMP = reg(40448, 512)
    C_GS = reg(40960, 512)
    C_UP = reg(41472, 512)
    MAXC_C = reg(41984, 64)

    rows = nc.alloc_sbuf_tensor("rows", [1, 4 * 1024], F32).ap()
    R = [rows[0:1, i * 1024:(i + 1) * 1024] for i in range(4)]
    rows_r = nc.alloc_sbuf_tensor("rows_r", [1, 3 * 1024], F32R).ap()
    RR = [rows_r[0:1, i * 1024:(i + 1) * 1024] for i in range(3)]

    bias_t = nc.alloc_sbuf_tensor("bias_t", [128, 112], F32).ap()
    IN_B, GATE_B = bias_t[:, 0:16], bias_t[:, 16:32]
    B_B, C_B, D_B = bias_t[:, 32:48], bias_t[:, 48:64], bias_t[:, 64:80]
    OUT_B, NW, ALOG = bias_t[:, 80:88], bias_t[:, 88:96], bias_t[:, 96:112]
    nnw_t = nc.alloc_sbuf_tensor("nnw", [128, 8], F32).ap()
    a_t = nc.alloc_sbuf_tensor("a_t", [128, 16], F32).ap()
    a_tmp = nc.alloc_sbuf_tensor("a_tmp", [128, 64], F32).ap()
    a_msk = nc.alloc_sbuf_tensor("a_msk", [128, 16], U32).ap()
    state_t = nc.alloc_sbuf_tensor("state", [128, 16], F32).ap()
    hprev_t = nc.alloc_sbuf_tensor("hprev", [128, 16], F32).ap()
    mask_t = nc.alloc_sbuf_tensor("maskt", [128, 1], F32).ap()
    ones_bf = nc.alloc_sbuf_tensor("ones_bf", [128, 2], BF16).ap()
    ones_r = nc.alloc_sbuf_tensor("ones_r", [1, 128], F32R).ap()
    ones_f = nc.alloc_sbuf_tensor("ones_f", [1, 128], F32).ap()
    onec_f = nc.alloc_sbuf_tensor("onec_f", [128, 2], F32).ap()
    ident = nc.alloc_sbuf_tensor("ident", [128, 128], F32).ap()
    mw_t = nc.alloc_sbuf_tensor("mw_t", [128, 8], F32).ap()
    wsc_t = nc.alloc_sbuf_tensor("wsc_t", [1, 8], F32).ap()
    acc_t = nc.alloc_sbuf_tensor("acc_t", [128, 8], F32).ap()
    inv1_row = nc.alloc_sbuf_tensor("inv1_row", [1, 512], F32).ap()

    psA = nc.alloc_psum_tensor("psA", [128, 2048], F32).ap()
    psB = nc.alloc_psum_tensor("psB", [128, 2048], F32).ap()

    es = ExitStack()
    semnames = ["sq", "aq", "gq", "tE", "vE", "sE", "cc"]
    sems = {k: es.enter_context(nc.semaphore(k)) for k in semnames}
    p = Prog(nc, sems)
    tap_state = {"done": False}

    def tap_out(name, ap, shape, dt=F32):
        if tap != name or tap_state["done"]:
            return
        tap_state["done"] = True
        th = nc.dram_tensor("dbg", list(shape), dt, kind="ExternalOutput").ap()
        p.join()
        p.dma("sync", "sq", th[:], ap)
        p.join()

    def emit_rsqrt(src, dst, scratch):
        p.run("scalar", lambda e: e.activation(scratch, src, AF.Ln), "sE")
        p.run("scalar", lambda e: e.activation(dst, scratch, AF.Exp, scale=-0.5), "sE")
        p.join(["vector"])
        for _ in range(2):
            p.run("vector", lambda e: e.tensor_tensor(scratch, dst, dst, ALU.mult), "vE")
            p.run("vector", lambda e: e.tensor_tensor(scratch, scratch, src, ALU.mult), "vE")
            p.run("vector", lambda e: e.tensor_scalar(
                scratch, scratch, -0.5, 1.5, ALU.mult, ALU.add), "vE")
            p.run("vector", lambda e: e.tensor_tensor(dst, dst, scratch, ALU.mult), "vE")

    def emit_stats(xb, xsq, var_mult, var_eps, n):
        for src, base in ((xb, 0), (xsq, 1024)):
            for k in range(DC):
                for h in range(n // 512):
                    p.run("tensor", lambda e, s=src, k=k, h=h, b=base: e.matmul(
                        psA[0:1, b + h * 512:b + (h + 1) * 512],
                        ones_bf[:, 0:1],
                        s[:, k * n + h * 512:k * n + (h + 1) * 512],
                        start=(k == 0), stop=(k == DC - 1)), "tE")
        p.wait("vector", "tE")
        p.run("vector", lambda e: e.tensor_scalar(
            R[0][0:1, 0:n], psA[0:1, 0:n], RCP1024, None, ALU.mult), "vE")
        p.run("vector", lambda e: e.tensor_scalar(
            R[1][0:1, 0:n], psA[0:1, 1024:1024 + n], RCP1024, None, ALU.mult), "vE")
        p.run("vector", lambda e: e.tensor_tensor(
            R[3][0:1, 0:n], R[0][0:1, 0:n], R[0][0:1, 0:n], ALU.mult), "vE")
        p.run("vector", lambda e: e.tensor_tensor(
            R[1][0:1, 0:n], R[1][0:1, 0:n], R[3][0:1, 0:n], ALU.subtract), "vE")
        p.run("vector", lambda e: e.tensor_scalar(
            R[1][0:1, 0:n], R[1][0:1, 0:n], var_mult, var_eps, ALU.mult, ALU.add), "vE")
        p.join(["scalar"])
        emit_rsqrt(R[1][0:1, 0:n], R[2][0:1, 0:n], R[3][0:1, 0:n])

    def emit_bcast(ps, pb_off, row_r, n):
        for h in range(n // 512):
            p.run("tensor", lambda e, h=h: e.matmul(
                ps[:, pb_off + h * 512:pb_off + (h + 1) * 512],
                ones_r[0:1, :], row_r[0:1, h * 512:(h + 1) * 512],
                start=True, stop=True), "tE")

    p.dma("sync", "sq", bias_t[:], bias_h[:])
    p.dma("sync", "sq", mask_t[:], mask_h[:])
    p.dma("sync", "sq", ident[:], ident_h[:])
    for (dst, src) in [(s_in, w_in_h), (s_gate, w_gate_h), (s_b, w_b_h),
                       (s_c, w_c_h), (s_d, w_d_h), (s_out, w_out_h)]:
        p.dma("gpsimd", "gq", dst[:], src[:])
    for mset in [ones_bf, ones_f, onec_f]:
        p.run("vector", lambda e, m=mset: e.memset(m[:], 1.0), "vE")
    p.run("vector", lambda e: e.tensor_copy(ones_r[:], ones_f[:]), "vE")
    p.run("vector", lambda e: e.memset(state_t[:], 0.0), "vE")
    p.run("vector", lambda e: e.memset(acc_t[:], 0.0), "vE")
    p.wait("vector", "sq", 16)
    p.run("vector", lambda e: e.tensor_scalar(nnw_t[:], NW, -1.0, None, ALU.mult), "vE")

    AT = [a_tmp[:, i * 16:(i + 1) * 16] for i in range(4)]
    p.join(["scalar"])
    p.run("scalar", lambda e: e.activation(AT[0], ALOG, AF.Abs), "sE")
    p.run("scalar", lambda e: e.activation(AT[0], AT[0], AF.Exp, scale=-1.0), "sE")
    p.run("scalar", lambda e: e.activation(AT[1], AT[0], AF.Ln, bias=1.0), "sE")
    p.join(["vector"])
    p.run("vector", lambda e: e.tensor_scalar(AT[2], AT[0], -0.25, 1.0 / 3.0, ALU.mult, ALU.add), "vE")
    p.run("vector", lambda e: e.tensor_tensor(AT[2], AT[2], AT[0], ALU.mult), "vE")
    p.run("vector", lambda e: e.tensor_scalar(AT[2], AT[2], -0.5, None, ALU.add), "vE")
    p.run("vector", lambda e: e.tensor_tensor(AT[2], AT[2], AT[0], ALU.mult), "vE")
    p.run("vector", lambda e: e.tensor_scalar(AT[2], AT[2], 1.0, None, ALU.add), "vE")
    p.run("vector", lambda e: e.tensor_tensor(AT[2], AT[2], AT[0], ALU.mult), "vE")
    p.run("vector", lambda e: e.tensor_scalar(a_msk[:], AT[0], 0.01, None, ALU.is_lt), "vE")
    p.run("vector", lambda e: e.select(AT[3], a_msk[:], AT[2], AT[1]), "vE")
    p.run("vector", lambda e: e.tensor_scalar(AT[0], ALOG, 0.0, None, ALU.max), "vE")
    p.run("vector", lambda e: e.tensor_tensor(AT[3], AT[3], AT[0], ALU.add), "vE")
    p.join(["scalar"])
    p.run("scalar", lambda e: e.activation(a_t[:], AT[3], AF.Exp, scale=-1.0), "sE")
    tap_out("a", a_t[:], [128, 16])

    ffn_tiles = []
    for (src, dst, mi) in [(w_fg_h, s_fg, 0), (w_fu_h, s_fu, 1), (w_fd_h, s_fd, 2)]:
        if mi < 2:
            for gi in range(8):
                ffn_tiles.append((src[gi], dst[gi], mi))
        else:
            for gi in range(2):
                for q in range(4):
                    ffn_tiles.append((src[gi][:, q * 4096:(q + 1) * 4096],
                                      dst[gi][:, q * 4096:(q + 1) * 4096], mi))
    prev_v = 0
    for i, (sap, dap, mi) in enumerate(ffn_tiles):
        par = i % 2
        if i >= 2:
            p.wait("sync", "vE", prev_v)
        lt = p.dma("sync", "sq", PT_F[par][:], sap)
        p.wait("vector", "sq", lt)
        p.run("vector", lambda e, t=PT_F[par]: e.reduce_sum(
            acc_t[:, 6:7], t[:], axis=AXX, apply_absolute_value=True), "vE")
        prev_v = p.run("vector", lambda e, m=mi: e.tensor_tensor(
            acc_t[:, m:m + 1], acc_t[:, m:m + 1], acc_t[:, 6:7], ALU.add), "vE")
    p.join(["tensor"])
    for mi in range(3):
        p.run("tensor", lambda e, m=mi: e.matmul(
            psA[0:1, m * 512:m * 512 + 1], onec_f[:, 0:1], acc_t[:, m:m + 1],
            start=True, stop=True), "tE")
    p.wait("vector", "tE")
    for mi in range(3):
        p.run("vector", lambda e, m=mi: e.tensor_scalar(
            wsc_t[0:1, m:m + 1], psA[0:1, m * 512:m * 512 + 1],
            float(np.float32(2.0 ** -22)), 1e-5, ALU.mult, ALU.max), "vE")
    p.join(["tensor"])
    for mi in range(3):
        p.run("tensor", lambda e, m=mi: e.matmul(
            psB[:, m * 512:m * 512 + 1], ones_f[0:1, :], wsc_t[0:1, m:m + 1],
            start=True, stop=True), "tE")
    p.wait("vector", "tE")
    for mi in range(3):
        p.run("vector", lambda e, m=mi: e.tensor_copy(
            mw_t[:, m:m + 1], psB[:, m * 512:m * 512 + 1]), "vE")
    for mi in range(3):
        p.run("vector", lambda e, m=mi: e.reciprocal(
            mw_t[:, 3 + m:4 + m], mw_t[:, m:m + 1]), "vE")
    p.join()

    prev_v = 0
    prev_g = 0
    for i, (sap, dap, mi) in enumerate(ffn_tiles):
        par = i % 2
        if i >= 2:
            p.wait("sync", "vE", prev_v)
        lt = p.dma("sync", "sq", PT_F[par][:], sap)
        p.wait("vector", "sq", lt)
        if i >= 2:
            p.wait("vector", "gq", prev_g)
        ws = mw_t[:, 3 + mi:4 + mi]
        p.run("vector", lambda e, t=PT_F[par], w=ws: e.tensor_scalar(
            t[:], t[:], w, CLIP15, ALU.mult, ALU.min), "vE")
        p.run("vector", lambda e, t=PT_F[par]: e.tensor_scalar(
            t[:], t[:], -CLIP15, CBIG, ALU.max, ALU.add), "vE")
        prev_v = p.run("vector", lambda e, t=PT_F[par], o=PT_B[par]: e.tensor_scalar(
            o[:], t[:], CBIG, None, ALU.subtract), "vE")
        p.wait("gpsimd", "vE", prev_v)
        prev_g = p.dma("gpsimd", "gq", dap, PT_B[par][:])
    p.join()

    x_r = x_h.rearrange("(c p) t -> p c t", p=128)
    X3 = X_T.rearrange("p (c t) -> p c t", c=DC)
    XB3 = XB_T.rearrange("p (c t) -> p c t", c=DC)

    def emit_ln1_head(s, need_xn, TMPA):
        xsl = x_r[:, :, s * SA:(s + 1) * SA]
        p.dma("sync", "sq", X3[:, :, :], xsl)
        p.dma("gpsimd", "gq", XB3[:, :, :], xsl)
        p.wait("scalar", "sq")
        p.wait("scalar", "gq")
        p.run("scalar", lambda e: e.activation(XSQ_T[:], X_T[:], AF.Square), "sE")
        p.wait("tensor", "gq")
        p.wait("tensor", "sE")
        if need_xn:
            emit_stats(XB_T, XSQ_T, 1.000001, 1e-11, SA)
            p.run("vector", lambda e: e.tensor_copy(RR[0][0:1, 0:SA], R[2][0:1, 0:SA]), "vE")
            p.run("vector", lambda e: e.tensor_tensor(
                R[3][0:1, 0:SA], R[0][0:1, 0:SA], R[2][0:1, 0:SA], ALU.mult), "vE")
            p.run("vector", lambda e: e.tensor_copy(RR[1][0:1, 0:SA], R[3][0:1, 0:SA]), "vE")
            p.join(["tensor"])
            emit_bcast(psB, 0, RR[0], SA)
            emit_bcast(psB, 1024, RR[1], SA)
            p.wait("vector", "tE")
            for c in range(DC):
                p.run("vector", lambda e, c=c: e.scalar_tensor_tensor(
                    TMPA[0][:], X_T[:, c * SA:(c + 1) * SA], NW[:, c:c + 1],
                    psB[:, 0:SA], ALU.mult, ALU.mult), "vE")
                p.run("vector", lambda e, c=c: e.scalar_tensor_tensor(
                    XN_T[:, c * SA:(c + 1) * SA], psB[:, 1024:1024 + SA],
                    nnw_t[:, c:c + 1], TMPA[0][:], ALU.mult, ALU.add), "vE")
        else:
            emit_stats(XB_T, XSQ_T, 1.0, 1e-5, SA)
            p.run("vector", lambda e: e.tensor_tensor(
                R[3][0:1, 0:SA], R[0][0:1, 0:SA], R[2][0:1, 0:SA], ALU.mult), "vE")
            p.run("vector", lambda e: e.tensor_copy(RR[1][0:1, 0:SA], R[3][0:1, 0:SA]), "vE")
            p.run("vector", lambda e: e.tensor_scalar(
                R[3][0:1, 0:SA], R[2][0:1, 0:SA], 1.0, None, ALU.add), "vE")
            p.run("vector", lambda e: e.tensor_copy(RR[2][0:1, 0:SA], R[3][0:1, 0:SA]), "vE")

    def emit_mm_loop(nm, kc, wsrc_fn, rhs, n, evac_fn):
        dmatok = [0] * nm
        mmtok = [0] * nm
        etok = [None] * nm
        for m in range(min(2, nm)):
            dmatok[m] = p.dma("sync", "sq", WS[m % 2][:, 0:kc * 128], wsrc_fn(m))
        for m in range(nm):
            par = m % 2
            ps = psA if par == 0 else psB
            p.wait("tensor", "sq", dmatok[m])
            if m >= 2 and etok[m - 2] is not None:
                p.wait("tensor", etok[m - 2][0], etok[m - 2][1])
            tok = None
            for k in range(kc):
                for h in range(n // 512):
                    tok = p.run("tensor", lambda e, ps=ps, k=k, h=h, par=par: e.matmul(
                        ps[:, h * 512:(h + 1) * 512],
                        WS[par][:, k * 128:(k + 1) * 128],
                        rhs[:, k * n + h * 512:k * n + (h + 1) * 512],
                        start=(k == 0), stop=(k == kc - 1)), "tE")
            mmtok[m] = tok
            if m + 2 < nm:
                p.wait("sync", "tE", mmtok[m])
                dmatok[m + 2] = p.dma("sync", "sq", WS[par][:, 0:kc * 128], wsrc_fn(m + 2))
            etok[m] = evac_fn(m, ps)
        return mmtok

    def emit_mm_loop2(nm, kc, wsrc1, wsrc2, rhs, n, evac_fn, pre_fn=None):
        dmatok = [[0] * nm, [0] * nm]
        mmtok = [0] * nm
        etok = [None] * nm
        for m in range(min(2, nm)):
            dmatok[0][m] = p.dma("sync", "sq", WS[m % 2][:, 0:kc * 128], wsrc1(m))
            dmatok[1][m] = p.dma("sync", "sq", WS[2 + m % 2][:, 0:kc * 128], wsrc2(m))
        for m in range(nm):
            par = m % 2
            ps = psA if par == 0 else psB
            if pre_fn is not None:
                pre_fn(m)
            p.wait("tensor", "sq", max(dmatok[0][m], dmatok[1][m]))
            if m >= 2 and etok[m - 2] is not None:
                p.wait("tensor", etok[m - 2][0], etok[m - 2][1])
            tok = None
            for wi, wb in ((0, WS[par]), (1, WS[2 + par])):
                for k in range(kc):
                    for h in range(n // 512):
                        tok = p.run("tensor", lambda e, ps=ps, k=k, h=h, wb=wb, wi=wi: e.matmul(
                            ps[:, wi * 1024 + h * 512:wi * 1024 + (h + 1) * 512],
                            wb[:, k * 128:(k + 1) * 128],
                            rhs[:, k * n + h * 512:k * n + (h + 1) * 512],
                            start=(k == 0), stop=(k == kc - 1)), "tE")
            mmtok[m] = tok
            if m + 2 < nm:
                p.wait("sync", "tE", mmtok[m])
                dmatok[0][m + 2] = p.dma("sync", "sq", WS[par][:, 0:kc * 128], wsrc1(m + 2))
                dmatok[1][m + 2] = p.dma("sync", "sq", WS[2 + par][:, 0:kc * 128], wsrc2(m + 2))
            etok[m] = evac_fn(m, ps)
        return mmtok

    for s in range(NS):
        p.join()
        emit_ln1_head(s, True, TMPA_A)
        tap_out("xn" if s == 0 else "_xn%d" % s, XN_T[:], [128, 8192], BF16)
        p.join()

        def evac_ug(m, ps):
            par = m % 2
            p.wait("scalar", "tE")
            p.run("scalar", lambda e, ps=ps, m=m, par=par: e.activation(
                TMPA_A[par][:], ps[:, 0:SA], AF.Identity, bias=IN_B[:, m:m + 1]), "sE")
            tok = p.run("scalar", lambda e, ps=ps, m=m, par=par: e.activation(
                BW[par][:], ps[:, 1024:1024 + SA], AF.Sigmoid, bias=GATE_B[:, m:m + 1]), "sE")
            p.wait("vector", "sE")
            p.run("vector", lambda e, m=m, par=par: e.tensor_tensor(
                U_T[:, m * SA:(m + 1) * SA], TMPA_A[par][:], BW[par][:], ALU.mult), "vE")
            return ("sE", tok)

        emit_mm_loop2(IC, DC, lambda m: s_in[m], lambda m: s_gate[m],
                      XN_T, SA, evac_ug)
        p.join()
        tap_out("u" if s == 0 else "_u%d" % s, U_T[:], [128, 16384], BF16)
        p.dma("gpsimd", "gq", u_sp[s], U_T[:])

        def evac_b(m, ps, s=s):
            par = m % 2
            if m >= 2:
                p.wait("scalar", "vE")
                p.wait("scalar", "gq")
            p.wait("scalar", "tE")
            tok = p.run("scalar", lambda e, ps=ps, m=m, par=par: e.activation(
                BW[par][:], ps[:, 0:SA], AF.Identity, bias=B_B[:, m:m + 1]), "sE")
            p.wait("gpsimd", "sE")
            p.dma("gpsimd", "gq", b_sp[s, m], BW[par][:])
            p.wait("vector", "sE")
            p.run("vector", lambda e, m=m, par=par: e.tensor_tensor_scan(
                SCO_A[:], a_t[:, m:m + 1].broadcast_to((128, SA)), BW[par][:],
                state_t[:, m:m + 1], ALU.mult, ALU.add), "vE")
            p.run("vector", lambda e, m=m: e.tensor_copy(
                state_t[:, m:m + 1], SCO_A[:, SA - 1:SA]), "vE")
            return ("sE", tok)

        emit_mm_loop(IC, IC, lambda m: s_b[m], U_T, SA, evac_b)
        p.join()
    tap_out("hend", state_t[:], [128, 16])

    p.join()
    p.dma("gpsimd", "gq", cc_in[:], state_t[:])
    p.wait("gpsimd", "gq")
    p.run("gpsimd", lambda e: e.collective_compute(
        "AllGather", ALU.bypass,
        replica_groups=CCGROUPS,
        ins=[cc_in[:]], outs=[cc_out[:]]), "cc")
    p.wait("gpsimd", "cc")
    p.dma("gpsimd", "gq", hprev_t[:], cc_out[0:128, :])
    p.wait("vector", "gq")
    p.run("vector", lambda e: e.tensor_scalar(
        state_t[:], hprev_t[:], mask_t[:, 0:1], None, ALU.mult), "vE")
    p.join()
    tap_out("hseed", state_t[:], [128, 16])

    for s in range(NS):
        p.join()
        emit_ln1_head(s, False, TMPA_B)
        p.join()
        p.dma("sync", "sq", U_T[:], u_sp[s])

        btok = {}

        def pre_cd(m, s=s, btok=btok):
            if m == 0:
                btok[0] = p.dma("sync", "aq", BW[0][:], b_sp[s, 0])
            if m + 1 < IC:
                p.wait("sync", "vE")
                btok[m + 1] = p.dma("sync", "aq", BW[(m + 1) % 2][:], b_sp[s, m + 1])
            p.wait("vector", "aq", btok[m])
            p.run("vector", lambda e, m=m: e.tensor_tensor_scan(
                SCO_B[:], a_t[:, m:m + 1].broadcast_to((128, SA)), BW[m % 2][:],
                state_t[:, m:m + 1], ALU.mult, ALU.add), "vE")
            p.run("vector", lambda e, m=m: e.tensor_copy(
                state_t[:, m:m + 1], SCO_B[:, SA - 1:SA]), "vE")

        def evac_cd(m, ps):
            par = m % 2
            p.wait("scalar", "tE")
            p.run("scalar", lambda e, ps=ps, m=m, par=par: e.activation(
                TMPA_B[par][:], ps[:, 0:SA], AF.Identity, bias=C_B[:, m:m + 1]), "sE")
            tok = p.run("scalar", lambda e, ps=ps, m=m: e.activation(
                TMP3_B[:], ps[:, 1024:1024 + SA], AF.Identity, bias=D_B[:, m:m + 1]), "sE")
            p.wait("vector", "sE")
            p.run("vector", lambda e, par=par: e.tensor_tensor(
                TMPA_B[par][:], TMPA_B[par][:], SCO_B[:], ALU.mult), "vE")
            p.run("vector", lambda e, m=m, par=par: e.tensor_tensor(
                Y_T[:, m * SA:(m + 1) * SA], TMPA_B[par][:], TMP3_B[:], ALU.add), "vE")
            return ("sE", tok)

        p.wait("tensor", "sq")
        emit_mm_loop2(IC, IC, lambda m: s_c[m], lambda m: s_d[m],
                      U_T, SA, evac_cd, pre_fn=pre_cd)
        p.join()
        tap_out("y" if s == 0 else "_y%d" % s, Y_T[:], [128, 16384], BF16)

        def evac_out(m, ps):
            p.wait("scalar", "tE")
            tok = p.run("scalar", lambda e, ps=ps, m=m: e.activation(
                X2_T[:, m * SA:(m + 1) * SA], ps[:, 0:SA], AF.Identity,
                bias=OUT_B[:, m:m + 1]), "sE")
            return ("sE", tok)

        emit_mm_loop(DC, IC, lambda m: s_out[m], Y_T, SA, evac_out)
        p.join()

        emit_bcast(psB, 0, RR[2], SA)
        emit_bcast(psB, 1024, RR[1], SA)
        p.wait("vector", "tE")
        for c in range(DC):
            p.run("vector", lambda e, c=c: e.tensor_tensor(
                TMPA_B[0][:], X_T[:, c * SA:(c + 1) * SA], psB[:, 0:SA], ALU.mult), "vE")
            p.run("vector", lambda e, c=c: e.scalar_tensor_tensor(
                TMPA_B[0][:], psB[:, 1024:1024 + SA], -1.0, TMPA_B[0][:],
                ALU.mult, ALU.add), "vE")
            p.run("vector", lambda e, c=c: e.tensor_tensor(
                X2_T[:, c * SA:(c + 1) * SA], X2_T[:, c * SA:(c + 1) * SA],
                TMPA_B[0][:], ALU.add), "vE")
        p.join()
        tap_out("x2" if s == 0 else "_x2%d" % s, X2_T[:], [128, 8192])
        p.dma("gpsimd", "gq", x2_sp[s], X2_T[:])

        p.run("vector", lambda e: e.tensor_copy(XB_T[:], X2_T[:]), "vE")
        p.run("scalar", lambda e: e.activation(XSQ_T[:], X2_T[:], AF.Square), "sE")
        p.join(["tensor"])
        emit_stats(XB_T, XSQ_T, 1.0, 1e-5, SA)
        p.run("vector", lambda e: e.tensor_copy(RR[0][0:1, 0:SA], R[0][0:1, 0:SA]), "vE")
        p.join(["tensor"])
        emit_bcast(psB, 0, RR[0], SA)
        p.wait("vector", "tE")
        for c in range(DC):
            p.run("vector", lambda e, c=c: e.scalar_tensor_tensor(
                X_T[:, c * SA:(c + 1) * SA], psB[:, 0:SA], -1.0,
                X2_T[:, c * SA:(c + 1) * SA], ALU.mult, ALU.add), "vE")
        p.join(["tensor"])
        slots = [(psA, 0), (psA, 1024), (psB, 0), (psB, 1024)]
        vtoks = {}
        for blk in range(SA // 128):
            for c in range(DC):
                i = blk * DC + c
                ps_s, sl = slots[i % 4]
                if i >= 4:
                    p.wait("tensor", "vE", vtoks[i - 4])
                p.run("tensor", lambda e, ps_s=ps_s, sl=sl, c=c, blk=blk: e.transpose(
                    ps_s[:, sl:sl + 128],
                    X_T[:, c * SA + blk * 128:c * SA + (blk + 1) * 128],
                    ident[:]), "tE")
                p.wait("vector", "tE")
                vtoks[i] = p.run("vector", lambda e, ps_s=ps_s, sl=sl, c=c: e.reduce_max(
                    MAXC[:, 8 + c:9 + c], ps_s[:, sl:sl + 128], axis=AXX,
                    apply_absolute_value=True), "vE")
            p.run("vector", lambda e, blk=blk: e.reduce_max(
                MAXC[:, blk:blk + 1], MAXC[:, 8:16], axis=AXX), "vE")
        p.join(["tensor"])
        for blk in range(SA // 128):
            ps_s, sl = slots[blk % 4]
            if blk >= 4:
                p.wait("tensor", "vE")
            p.run("tensor", lambda e, ps_s=ps_s, sl=sl, blk=blk: e.transpose(
                ps_s[0:1, sl:sl + 128], MAXC[:, blk:blk + 1], ident[:]), "tE")
            p.wait("vector", "tE")
            p.run("vector", lambda e, ps_s=ps_s, sl=sl, blk=blk: e.tensor_copy(
                R[3][0:1, blk * 128:(blk + 1) * 128], ps_s[0:1, sl:sl + 128]), "vE")
        p.run("vector", lambda e: e.tensor_tensor(
            R[3][0:1, 0:SA], R[3][0:1, 0:SA], R[2][0:1, 0:SA], ALU.mult), "vE")
        p.run("vector", lambda e: e.tensor_scalar(
            R[3][0:1, 0:SA], R[3][0:1, 0:SA], 1e-5, None, ALU.max), "vE")
        p.run("vector", lambda e: e.tensor_scalar(
            R[1][0:1, 0:SA], R[3][0:1, 0:SA], float(np.float32(1.0 / 127.0)),
            None, ALU.mult), "vE")
        p.wait("gpsimd", "vE")
        sp_tok = p.dma("gpsimd", "gq", inv1_sp[s:s + 1, :], R[1][0:1, 0:SA])
        p.wait("vector", "gq", sp_tok)
        p.run("vector", lambda e: e.reciprocal(R[1][0:1, 0:SA], R[3][0:1, 0:SA]), "vE")
        p.run("vector", lambda e: e.tensor_scalar(
            R[1][0:1, 0:SA], R[1][0:1, 0:SA], 127.0, None, ALU.mult), "vE")
        p.run("vector", lambda e: e.tensor_tensor(
            R[1][0:1, 0:SA], R[1][0:1, 0:SA], R[2][0:1, 0:SA], ALU.mult), "vE")
        p.run("vector", lambda e: e.tensor_copy(RR[1][0:1, 0:SA], R[1][0:1, 0:SA]), "vE")
        p.join(["tensor"])
        emit_bcast(psB, 0, RR[1], SA)
        p.wait("vector", "tE")
        for c in range(DC):
            p.run("vector", lambda e, c=c: e.tensor_tensor(
                YQF_T[:, c * SA:(c + 1) * SA], X_T[:, c * SA:(c + 1) * SA],
                psB[:, 0:SA], ALU.mult), "vE")
            p.run("vector", lambda e, c=c: e.tensor_scalar(
                XQ1_T[:, c * SA:(c + 1) * SA], YQF_T[:, c * SA:(c + 1) * SA],
                CBIG, CBIG, ALU.add, ALU.subtract), "vE")
        p.join(["gpsimd"])
        p.dma("gpsimd", "gq", xq1_sp[s], XQ1_T[:])
        tap_out("xq1" if s == 0 else "_xq1%d" % s, XQ1_T[:], [128, 8192], BF16)

    for sc in range(NSC):
        s, t0 = sc // 2, (sc % 2) * SC
        p.join()
        xq1s = xq1_sp[s].rearrange("p (c t) -> p c t", c=DC)[:, :, t0:t0 + SC]
        x2s = x2_sp[s].rearrange("p (c t) -> p c t", c=DC)[:, :, t0:t0 + SC]
        p.dma("sync", "sq", C_XQ1.rearrange("p (c t) -> p c t", c=DC), xq1s)
        p.dma("sync", "sq", C_X2.rearrange("p (c t) -> p c t", c=DC), x2s)
        p.dma("sync", "aq", inv1_row[0:1, 0:SC], inv1_sp[s:s + 1, t0:t0 + SC])
        p.wait("vector", "aq")
        p.run("vector", lambda e: e.tensor_copy(RR[0][0:1, 0:SC], inv1_row[0:1, 0:SC]), "vE")
        p.join(["tensor"])
        emit_bcast(psA, 0, RR[0], SC)
        p.wait("vector", "tE")
        p.run("vector", lambda e: e.tensor_copy(C_BC[:, 0:SC], psA[:, 0:SC]), "vE")
        p.join()

        gdma = [[0] * 8, [0] * 8]
        for gi in range(2):
            gdma[0][gi] = p.dma("sync", "sq", C_WF[gi % 2][:], s_fg[gi])
            gdma[1][gi] = p.dma("sync", "aq", C_WF[2 + gi % 2][:], s_fu[gi])
        gtok = 0
        utok = 0
        for gi in range(8):
            par = gi % 2
            p.wait("tensor", "sq", gdma[0][gi])
            tok = None
            for wb, ps, rd in ((C_WF[par], psA, gtok), (C_WF[2 + par], psB, utok)):
                if gi >= 1 and rd:
                    p.wait("tensor", "vE", rd)
                if ps is psB:
                    p.wait("tensor", "aq", gdma[1][gi])
                for k in range(DC):
                    for ms in range(4):
                        tok = p.run("tensor", lambda e, ps=ps, k=k, ms=ms, wb=wb: e.matmul(
                            ps[:, ms * 512:(ms + 1) * 512],
                            wb[:, k * 512 + ms * 128:k * 512 + (ms + 1) * 128],
                            C_XQ1[:, k * SC:(k + 1) * SC],
                            start=(k == 0), stop=(k == DC - 1)), "tE")
                if ps is psA:
                    tok_gate = tok
            if gi + 2 < 8:
                p.wait("sync", "tE", tok)
                p.wait("sync", "vE", utok)
                gdma[0][gi + 2] = p.dma("sync", "sq", C_WF[par][:], s_fg[gi + 2])
                gdma[1][gi + 2] = p.dma("sync", "aq", C_WF[2 + par][:], s_fu[gi + 2])
            p.wait("vector", "tE", tok_gate)
            for ms in range(4):
                gtok = p.run("vector", lambda e, gi=gi, ms=ms: e.scalar_tensor_tensor(
                    C_GU[:, gi * 2048 + ms * 512:gi * 2048 + (ms + 1) * 512],
                    psA[:, ms * 512:(ms + 1) * 512],
                    mw_t[:, 0:1], C_BC[:, 0:SC], ALU.mult, ALU.mult), "vE")
            p.wait("vector", "tE", tok)
            for ms in range(4):
                utok = p.run("vector", lambda e, ms=ms: e.scalar_tensor_tensor(
                    C_UPST[:, ms * 512:(ms + 1) * 512],
                    psB[:, ms * 512:(ms + 1) * 512],
                    mw_t[:, 1:2], C_BC[:, 0:SC], ALU.mult, ALU.mult), "vE")
            for ms in range(4):
                p.wait("scalar", "vE")
                p.run("scalar", lambda e, gi=gi, ms=ms: e.activation(
                    C_GS[:], C_GU[:, gi * 2048 + ms * 512:gi * 2048 + (ms + 1) * 512],
                    AF.Silu), "sE")
                p.wait("vector", "sE")
                p.run("vector", lambda e, gi=gi, ms=ms: e.tensor_tensor(
                    C_GU[:, gi * 2048 + ms * 512:gi * 2048 + (ms + 1) * 512],
                    C_GS[:], C_UPST[:, ms * 512:(ms + 1) * 512], ALU.mult), "vE")
        p.join()
        tap_out("gu" if sc == 0 else "_gu%d" % sc, C_GU[:], [128, 16384])

        slots = [(psA, 0), (psA, 1024), (psB, 0), (psB, 1024)]
        vtoks = {}
        for blk in range(SC // 128):
            for fc in range(FC):
                i = blk * FC + fc
                ps_s, sl = slots[i % 4]
                if i >= 4:
                    p.wait("tensor", "vE", vtoks[i - 4])
                p.run("tensor", lambda e, ps_s=ps_s, sl=sl, fc=fc, blk=blk: e.transpose(
                    ps_s[:, sl:sl + 128],
                    C_GU[:, fc * SC + blk * 128:fc * SC + (blk + 1) * 128],
                    ident[:]), "tE")
                p.wait("vector", "tE")
                vtoks[i] = p.run("vector", lambda e, ps_s=ps_s, sl=sl, fc=fc: e.reduce_max(
                    MAXC_C[:, 16 + fc:17 + fc], ps_s[:, sl:sl + 128],
                    axis=AXX, apply_absolute_value=True), "vE")
            p.run("vector", lambda e, blk=blk: e.reduce_max(
                MAXC_C[:, blk:blk + 1], MAXC_C[:, 16:48], axis=AXX), "vE")
        p.join(["tensor"])
        for blk in range(SC // 128):
            ps_s, sl = slots[blk % 4]
            if blk >= 4:
                p.wait("tensor", "vE")
            p.run("tensor", lambda e, ps_s=ps_s, sl=sl, blk=blk: e.transpose(
                ps_s[0:1, sl:sl + 128], MAXC_C[:, blk:blk + 1], ident[:]), "tE")
            p.wait("vector", "tE")
            p.run("vector", lambda e, ps_s=ps_s, sl=sl, blk=blk: e.tensor_copy(
                R[3][0:1, blk * 128:(blk + 1) * 128], ps_s[0:1, sl:sl + 128]), "vE")
        p.run("vector", lambda e: e.tensor_scalar(
            R[3][0:1, 0:SC], R[3][0:1, 0:SC], 1e-5, None, ALU.max), "vE")
        p.run("vector", lambda e: e.tensor_scalar(
            R[1][0:1, 0:SC], R[3][0:1, 0:SC], float(np.float32(1.0 / 127.0)),
            None, ALU.mult), "vE")
        p.run("vector", lambda e: e.tensor_copy(RR[1][0:1, 0:SC], R[1][0:1, 0:SC]), "vE")
        p.run("vector", lambda e: e.reciprocal(R[1][0:1, 0:SC], R[3][0:1, 0:SC]), "vE")
        p.run("vector", lambda e: e.tensor_scalar(
            R[1][0:1, 0:SC], R[1][0:1, 0:SC], 127.0, None, ALU.mult), "vE")
        p.run("vector", lambda e: e.tensor_copy(RR[2][0:1, 0:SC], R[1][0:1, 0:SC]), "vE")
        p.join(["tensor"])
        emit_bcast(psA, 0, RR[2], SC)
        emit_bcast(psA, 512, RR[1], SC)
        p.wait("vector", "tE")
        p.run("vector", lambda e: e.tensor_copy(C_BC[:, 512:1024], psA[:, 0:512]), "vE")
        p.run("vector", lambda e: e.tensor_copy(C_BC[:, 1024:1536], psA[:, 512:1024]), "vE")
        for fc in range(FC):
            p.run("vector", lambda e, fc=fc: e.tensor_tensor(
                C_UP[:], C_GU[:, fc * SC:(fc + 1) * SC], C_BC[:, 512:1024],
                ALU.mult), "vE")
            p.run("vector", lambda e, fc=fc: e.tensor_scalar(
                C_XQ2[:, fc * SC:(fc + 1) * SC], C_UP[:],
                CBIG, CBIG, ALU.add, ALU.subtract), "vE")
        p.join()

        wdma = {(0, 0): p.dma("sync", "sq", C_WD[0][:], s_fd[0][:, 0:8192]),
                (0, 1): p.dma("sync", "sq", C_WD[1][:], s_fd[0][:, 8192:16384])}
        for dg in range(2):
            ps = psA if dg == 0 else psB
            tok = None
            for kh in range(2):
                p.wait("tensor", "sq", wdma[(dg, kh)])
                for k in range(16):
                    gk = kh * 16 + k
                    for ms in range(4):
                        tok = p.run("tensor", lambda e, ps=ps, k=k, ms=ms, kh=kh, gk=gk: e.matmul(
                            ps[:, ms * 512:(ms + 1) * 512],
                            C_WD[kh][:, k * 512 + ms * 128:k * 512 + (ms + 1) * 128],
                            C_XQ2[:, gk * SC:(gk + 1) * SC],
                            start=(gk == 0), stop=(gk == FC - 1)), "tE")
            if dg == 0:
                p.wait("sync", "tE", tok)
                wdma[(1, 0)] = p.dma("sync", "sq", C_WD[0][:], s_fd[1][:, 0:8192])
                wdma[(1, 1)] = p.dma("sync", "sq", C_WD[1][:], s_fd[1][:, 8192:16384])
            p.wait("vector", "tE", tok)
            for ms in range(4):
                c = dg * 4 + ms
                p.run("vector", lambda e, ps=ps, ms=ms: e.scalar_tensor_tensor(
                    C_TMP[:], ps[:, ms * 512:(ms + 1) * 512], mw_t[:, 2:3],
                    C_BC[:, 1024:1536], ALU.mult, ALU.mult), "vE")
                p.run("vector", lambda e, c=c: e.tensor_tensor(
                    C_X2[:, c * SC:(c + 1) * SC], C_X2[:, c * SC:(c + 1) * SC],
                    C_TMP[:], ALU.add), "vE")
        p.join(["sync"])
        out_sl = out_h.rearrange("(c p) t -> p c t", p=128)[:, :, sc * SC:(sc + 1) * SC]
        p.dma("sync", "sq", out_sl, C_X2.rearrange("p (c t) -> p c t", c=DC))

    p.join()
    p.wait("sync", "sq", p.ct["sq"])

    with nc.Block() as block:
        p.replay(block)
    es.close()
    return nc


def _tile_w(wT, kc, mc, mw=128):
    A = wT.reshape(kc, 128, mc, mw).transpose(2, 1, 0, 3)
    return np.ascontiguousarray(A.reshape(mc, 128, kc * mw)).astype(np.float32)


def _col_pack(v, ncol):
    return np.ascontiguousarray(v.reshape(ncol, 128).T).astype(np.float32)


def kernel(**inputs):
    g = {k: np.asarray(v, dtype=np.float32) for k, v in inputs.items()}
    x = g["L0_attn_x"]

    shared = {
        "w_in": _tile_w(np.ascontiguousarray(g["L0_attn_in_w"].T), 8, 16),
        "w_gate": _tile_w(np.ascontiguousarray(g["L0_attn_gate_w"].T), 8, 16),
        "w_b": _tile_w(np.ascontiguousarray(g["L0_attn_b_w"].T), 16, 16),
        "w_c": _tile_w(np.ascontiguousarray(g["L0_attn_c_w"].T), 16, 16),
        "w_d": _tile_w(np.ascontiguousarray(g["L0_attn_d_w"].T), 16, 16),
        "w_out": _tile_w(np.ascontiguousarray(g["L0_attn_out_w"].T), 16, 8),
        "w_fg": _tile_w(np.ascontiguousarray(g["L0_ffn_w_gate"].T), 8, 8, 512),
        "w_fu": _tile_w(np.ascontiguousarray(g["L0_ffn_w_up"].T), 8, 8, 512),
        "w_fd": _tile_w(np.ascontiguousarray(g["L0_ffn_w_down"].T), 32, 2, 512),
        "ident_in": np.eye(128, dtype=np.float32),
    }
    bias = np.zeros((128, 112), np.float32)
    bias[:, 0:16] = _col_pack(g["L0_attn_in_b"], 16)
    bias[:, 16:32] = _col_pack(g["L0_attn_gate_b"], 16)
    bias[:, 32:48] = _col_pack(g["L0_attn_b_b"], 16)
    bias[:, 48:64] = _col_pack(g["L0_attn_c_b"], 16)
    bias[:, 64:80] = _col_pack(g["L0_attn_d_b"], 16)
    bias[:, 80:88] = _col_pack(g["L0_attn_out_b"], 8)
    bias[:, 88:96] = _col_pack(g["L0_attn_norm_w"], 8)
    bias[:, 96:112] = _col_pack(g["L0_attn_a_log"], 16)
    shared["biasp"] = bias

    in_maps = []
    for c in range(8):
        bc, h = c // 2, c % 2
        m = dict(shared)
        m["x_fm"] = np.ascontiguousarray(x[bc, h * T:(h + 1) * T, :].T)
        m["maskc"] = np.full((128, 1), float(h), np.float32)
        in_maps.append(m)

    nc = build_nc()
    res = run_bass_kernel_spmd(nc, in_maps, core_ids=list(range(8)))
    out = np.empty((4, 8192, 1024), np.float32)
    for c in range(8):
        bc, h = c // 2, c % 2
        out[bc, h * T:(h + 1) * T, :] = np.asarray(res.results[c]["out_fm"]).T
    return out
